# version 05
# baseline (speedup 1.0000x reference)
import contextlib
import numpy as np
import ml_dtypes
import concourse.bass as bass
import concourse.mybir as mybir
from concourse.bass_utils import run_bass_kernel_spmd

F32 = mybir.dt.float32
BF16 = mybir.dt.bfloat16
AF = mybir.ActivationFunctionType
OP = mybir.AluOpType
BF = ml_dtypes.bfloat16

NCORES = 8
S = 4096
D = 2048
DQKV = 6144
DFF = 8192
T = 512
NT = S // T
TBN = T // 128
NG = 4
EPS_RMS = float(np.float32(1.1920929e-07))
EPS_LN = 1e-5
EPS_RATIO = 1e-6
RS2048 = float(1.0 / np.sqrt(np.float64(2048.0)))
RS8192 = float(1.0 / np.sqrt(np.float64(8192.0)))

DEBUG_TAPS = ()
SINGLE_PRE = False
SINGLE_CROSS = True
SINGLE_WITHIN = False


def _hadamard(n):
    h = np.array([[1.0]], dtype=np.float64)
    while h.shape[0] < n:
        h = np.block([[h, h], [h, -h]])
    return h


def _host_shared(inputs):
    d = {}
    Wqkv = np.ascontiguousarray(inputs["L0_attn_w_qkv"], dtype=np.float32)
    Wsw = np.ascontiguousarray(inputs["L0_attn_w_swiglu"], dtype=np.float32)
    Wout = np.ascontiguousarray(inputs["L0_attn_w_out"], dtype=np.float32)
    Wup = np.ascontiguousarray(inputs["L0_ffn_W_up"], dtype=np.float32)
    Wgate = np.ascontiguousarray(inputs["L0_ffn_W_gate"], dtype=np.float32)
    Wdown = np.ascontiguousarray(inputs["L0_ffn_W_down"], dtype=np.float32)

    d["hq"] = np.ascontiguousarray(
        Wqkv.reshape(12, 512, 16, 128).transpose(0, 3, 2, 1))
    d["hs"] = np.ascontiguousarray(
        Wsw.reshape(128, 128, 16, 128).transpose(0, 3, 2, 1))
    d["hw"] = np.ascontiguousarray(
        Wout.reshape(4, 512, 8, 8, 128).transpose(0, 2, 4, 3, 1))
    for nm, W in (("hu", Wup), ("hg", Wgate)):
        d[nm] = np.ascontiguousarray(
            W.T.reshape(16, 16, 8, DFF).transpose(0, 2, 1, 3).reshape(128, 16, DFF))
    d["hd"] = np.ascontiguousarray(
        Wdown.T.reshape(64, 64, 2, D).transpose(0, 2, 1, 3).reshape(128, 64, D))

    SU_ug = np.asarray(inputs["L0_ffn_SU_ug"], np.float32)
    SU_dp = np.asarray(inputs["L0_ffn_SU_dp"], np.float32)
    Wsc_ug = np.asarray(inputs["L0_ffn_Wscale_ug"], np.float32)
    Wsc_dp = np.asarray(inputs["L0_ffn_Wscale_dp"], np.float32)
    d["su_ug"] = np.ascontiguousarray(SU_ug.reshape(16, 128).T)
    d["su_dp"] = np.ascontiguousarray(SU_dp.reshape(64, 128).T)
    d["wsu_raw"] = np.ascontiguousarray(Wsc_ug[:DFF].reshape(64, 128).T)
    d["wsg"] = np.ascontiguousarray(Wsc_ug[DFF:].reshape(64, 128).T)
    d["wsd_row"] = np.ascontiguousarray(Wsc_dp.reshape(1, D))

    bfc = lambda a: np.ascontiguousarray(np.asarray(a, np.float32)).astype(BF)
    d["iden"] = bfc(np.eye(128))
    d["idenf"] = np.ascontiguousarray(np.eye(128, dtype=np.float32))
    d["constv"] = np.ascontiguousarray(np.broadcast_to(np.array(
        [[1e-5, float(np.float32(1.1920929e-07)), 1.0 / 2048.0, 1.0]],
        np.float32), (128, 4)).copy())
    d["onesq"] = bfc(np.ones((128, 128)))
    d["ones_col"] = bfc(np.ones((128, 1)))
    d["ones_row"] = bfc(np.ones((1, 128)))
    d["ones_row_f"] = np.ones((1, 128), np.float32)
    d["utri"] = bfc(np.triu(np.ones((128, 128))))
    d["h128"] = bfc(_hadamard(128))
    d["h16i8"] = bfc(np.kron(_hadamard(16), np.eye(8)))
    d["h64i2"] = bfc(np.kron(_hadamard(64), np.eye(2)))
    return d



class Prog:
    ENGS = ("sync", "tensor", "vector", "scalar", "gpsimd")

    def __init__(self):
        self.q = {e: [] for e in self.ENGS}
        self.cnt = {}
        self.waited = {e: {} for e in self.ENGS}

    def wait(self, eng, ev):
        if ev is None:
            return
        if isinstance(ev, list):
            for e in ev:
                self.wait(eng, e)
            return
        sem, val = ev
        if val <= 0 or self.waited[eng].get(sem, 0) >= val:
            return
        self.waited[eng][sem] = val
        self.q[eng].append(("w", sem, val))

    def op(self, eng, fn):
        self.q[eng].append(("o", fn, None, 0))

    def inc_last(self, eng, sem, n=1):
        tag, fn, inc, _ = self.q[eng][-1]
        assert tag == "o" and inc is None
        self.cnt[sem] = self.cnt.get(sem, 0) + n
        self.q[eng][-1] = ("o", fn, sem, n)
        return (sem, self.cnt[sem])

    def opi(self, eng, fn, sem, n=1):
        self.op(eng, fn)
        return self.inc_last(eng, sem, n)

    def val(self, sem):
        return (sem, self.cnt.get(sem, 0))



def build_nc():
    nc = bass.Bass("TRN2", target_bir_lowering=False, debug=False)
    c = type("C", (), {})()
    c.nc = nc

    di = lambda n, sh, dt=F32: nc.dram_tensor(n, sh, dt, kind="ExternalInput").ap()
    c.x_fm = di("x_fm", [D, S])
    c.x_tm = di("x_tm", [S, D])
    c.hq = di("hq", [12, 128, 16, 512])
    c.hs = di("hs", [128, 128, 16, 128])
    c.hw = di("hw", [4, 8, 128, 8, 512])
    c.hu = di("hu", [128, 16, DFF])
    c.hg = di("hg", [128, 16, DFF])
    c.hd = di("hd", [128, 64, D])
    c.su_ug = di("su_ug", [128, 16])
    c.su_dp = di("su_dp", [128, 64])
    c.wsu_raw = di("wsu_raw", [128, 64])
    c.wsg_in = di("wsg", [128, 64])
    c.wsd_row = di("wsd_row", [1, D])
    c.iden = di("iden", [128, 128], BF16)
    c.idenf = di("idenf", [128, 128])
    c.constv = di("constv", [128, 4])
    c.onesq = di("onesq", [128, 128], BF16)
    c.ones_col = di("ones_col", [128, 1], BF16)
    c.ones_row = di("ones_row", [1, 128], BF16)
    c.ones_row_f = di("ones_row_f", [1, 128])
    c.utri = di("utri", [128, 128], BF16)
    c.h128 = di("h128", [128, 128], BF16)
    c.h16i8 = di("h16i8", [128, 128], BF16)
    c.h64i2 = di("h64i2", [128, 128], BF16)

    c.out = nc.dram_tensor("out", [S, D], F32, kind="ExternalOutput").ap()
    c.T1u = nc.dram_tensor("T1u", [D, DFF], BF16, kind="Internal").ap()
    c.T1g = nc.dram_tensor("T1g", [D, DFF], BF16, kind="Internal").ap()
    c.T1d = nc.dram_tensor("T1d", [DFF, D], BF16, kind="Internal").ap()
    c.WugS = nc.dram_tensor("WugS", [64, 128, 2, 16, 128], BF16, kind="Internal").ap()
    c.WoS = nc.dram_tensor("WoS", [4, 8, 128, 8, 512], BF16, kind="Internal").ap()
    c.WdS = nc.dram_tensor("WdS", [4, 8, 128, 8, 512], BF16, kind="Internal").ap()
    c.taps = {}
    shapes = {"xn": ([D, S], BF16), "qkv": ([S, DQKV], BF16),
              "y": ([D, S], BF16), "h": ([DFF, S], BF16),
              "mlp": ([S, D], BF16), "f": ([DFF, S], BF16),
              "cols": ([NT, 128, 8, TBN], F32)}
    shapes["t1u"] = ([D, DFF], BF16)
    shapes["wug"] = ([64, 128, 2, 16, 128], BF16)
    for t in DEBUG_TAPS:
        sh, dt = shapes[t]
        c.taps[t] = nc.dram_tensor("tap_" + t, sh, dt, kind="ExternalOutput").ap()

    SB = 16640
    A = lambda name, shape, dtype, offset: nc.alloc_sbuf_tensor_at(
        name, shape, dtype, offset=offset + SB)
    c.qkv_bf = A("qkv_bf", [128, TBN, DQKV], BF16, offset=0)
    c.wg_w = A("wg_w", [128, 2, TBN, 512], BF16, offset=49152)
    c.wg_wv = A("wg_wv", [128, 2, TBN, 512], BF16, offset=57344)
    c.h_fm = A("h_fm", [128, 64, T], BF16, offset=0)
    c.pre_in0 = A("pre_in0", [128, 8192], BF16, offset=0)
    c.pre_in1 = A("pre_in1", [128, 8192], BF16, offset=16384)
    c.pre_out0 = A("pre_out0", [128, 8192], BF16, offset=32768)
    c.pre_out1 = A("pre_out1", [128, 8192], BF16, offset=49152)
    c.pre_in0b = A("pre_in0b", [128, 8192], BF16, offset=65536)
    c.pre_out0b = A("pre_out0b", [128, 8192], BF16, offset=81920)
    c.pre_in1b = A("pre_in1b", [128, 8192], BF16, offset=98304)
    c.pre_out1b = A("pre_out1b", [128, 8192], BF16, offset=147456)
    c.x_tm_rot = A("x_tm_rot", [128, 2, D], F32, offset=0)
    c.oa_a = A("oa_a", [128, D], F32, offset=16384)
    c.oa_b = A("oa_b", [128, D], F32, offset=24576)
    c.wsd_tile = A("wsd_tile", [128, D], F32, offset=32768)
    c.x_fm_s = A("x_fm_s", [128, 16, T], F32, offset=65536)
    c.bm = A("bm", [128, T], F32, offset=98304)
    c.bs = A("bs", [128, T], F32, offset=100352)
    c.l_t1 = A("l_t1", [128, T], F32, offset=102400)
    c.l_t2 = A("l_t2", [128, T], F32, offset=104448)
    c.l_t3 = A("l_t3", [128, T], F32, offset=106496)
    c.l_t4 = A("l_t4", [128, T], F32, offset=108544)
    c.f1t = A("f1t", [128, T], F32, offset=110592)
    c.s1pt = A("s1pt", [128, T], F32, offset=112640)
    c.msnt = A("msnt", [128, T], F32, offset=114688)
    c.xc_rot = A("xc_rot", [128, 2, T], BF16, offset=116736)
    c.xs_rot = A("xs_rot", [128, 2, T], BF16, offset=118784)
    c.dn_rot = A("dn_rot", [128, 2, T], F32, offset=65536)
    c.rec_rot = A("rec_rot", [128, 2, T], F32, offset=69632)
    c.sig_rot = A("sig_rot", [128, 2, T], F32, offset=73728)
    c.t2_rot = A("t2_rot", [128, 2, T], F32, offset=77824)
    c.y_rot = A("y_rot", [128, 2, T], BF16, offset=81920)
    c.f_fm = A("f_fm", [128, 64, T], BF16, offset=65536)
    c.hsu = A("hsu", [128, 16, 128], BF16, offset=120832)
    c.pre_misc = A("pre_misc", [128, 128], BF16, offset=125184)
    c.warena = A("warena", [128, 2, 4096], BF16, offset=131072)
    c.xn_bf = A("xn_bf", [128, 16, T], BF16, offset=147456)
    c.qsq = A("qsq", [128, D], BF16, offset=163840)
    c.y_fm = A("y_fm", [128, 16, T], BF16, offset=163840)
    c.mlp_bf = A("mlp_bf", [128, TBN, D], BF16, offset=163840)
    c.car_kv = A("car_kv", [1, D], BF16, offset=180224)
    c.car_w = A("car_w", [1, D], BF16, offset=184320)
    c.silu_rot = A("silu_rot", [128, 2, 512], F32, offset=188416)
    off = [192512]

    def A7(name, shape, dtype):
        t = A(name, shape, dtype, offset=off[0])
        nbytes = int(np.prod(shape[1:])) * (2 if dtype == BF16 else 4)
        off[0] += (nbytes + 31) // 32 * 32
        return t
    c.s_iden = A7("s_iden", [128, 128], BF16)
    c.s_idenf = A7("s_idenf", [128, 128], F32)
    c.s_constv = A7("s_constv", [128, 4], F32)
    c.s_onesq = A7("s_onesq", [128, 128], BF16)
    c.s_onesc = A7("s_onesc", [128, 1], BF16)
    c.s_onesr = A7("s_onesr", [1, 128], BF16)
    c.s_onesrf = A7("s_onesrf", [1, 128], F32)
    c.s_utri = A7("s_utri", [128, 128], BF16)
    c.s_h128 = A7("s_h128", [128, 128], BF16)
    c.s_h16i8 = A7("s_h16i8", [128, 128], BF16)
    c.s_h64i2 = A7("s_h64i2", [128, 128], BF16)
    c.s_suug = A7("s_suug", [128, 16], F32)
    c.s_sudp = A7("s_sudp", [128, 64], F32)
    c.s_wsuraw = A7("s_wsuraw", [128, 64], F32)
    c.s_wsg = A7("s_wsg", [128, 64], F32)
    c.s_wsu = A7("s_wsu", [128, 64], F32)
    c.s_wsdr = A7("s_wsdr", [1, D], F32)
    c.col_sx = A7("col_sx", [128, TBN], F32)
    c.col_sx2 = A7("col_sx2", [128, TBN], F32)
    c.col_m = A7("col_m", [128, TBN], F32)
    c.col_t = A7("col_t", [128, TBN], F32)
    c.col_var = A7("col_var", [128, TBN], F32)
    c.col_u1 = A7("col_u1", [128, TBN], F32)
    c.col_f1 = A7("col_f1", [128, TBN], F32)
    c.col_s = A7("col_s", [128, TBN], F32)
    c.col_s1p = A7("col_s1p", [128, TBN], F32)
    c.col_msn = A7("col_msn", [128, TBN], F32)
    c.col_sq = A7("col_sq", [128, TBN], F32)
    c.col_sk = A7("col_sk", [128, TBN], F32)
    c.row_rot = A7("row_rot", [1, 512], F32)
    c.col_tq = A7("col_tq", [128, TBN], F32)
    c.col_tk = A7("col_tk", [128, TBN], F32)
    c.col_dbg = A7("col_dbg", [128, 8, TBN], F32)
    assert off[0] <= 212736, f"R7 overflow: {off[0]}"

    nc.const_aps.register(F32, EPS_LN, c.s_constv[:, 0:1])
    nc.const_aps.register(F32, EPS_RMS, c.s_constv[:, 1:2])
    nc.const_aps.register(F32, 1.0 / D, c.s_constv[:, 2:3])
    nc.const_aps.register(F32, 1.0, c.s_constv[:, 3:4])
    c.pb = [nc.alloc_psum_tensor(f"pb{i}", [128, 512], F32) for i in range(8)]
    c.pbb = [c.pb[i][:].bitcast(BF16) for i in range(8)]

    c.prog = Prog()
    return c


def emit_preamble(c):
    p = c.prog

    consts = [(c.s_iden, c.iden), (c.s_idenf, c.idenf), (c.s_constv, c.constv),
              (c.s_onesq, c.onesq), (c.s_onesc, c.ones_col),
              (c.s_onesr, c.ones_row), (c.s_onesrf, c.ones_row_f),
              (c.s_utri, c.utri), (c.s_h128, c.h128), (c.s_h16i8, c.h16i8),
              (c.s_h64i2, c.h64i2), (c.s_suug, c.su_ug), (c.s_sudp, c.su_dp),
              (c.s_wsuraw, c.wsu_raw), (c.s_wsg, c.wsg_in), (c.s_wsdr, c.wsd_row)]
    for dst, src in consts:
        p.op("sync", lambda e, d=dst, s=src: e.dma_start(out=d[:], in_=s))
        ev_c = p.inc_last("sync", "dK", 16)
    c.ev_consts = ev_c

    for ns in range(4):
        p.op("gpsimd", lambda e, ns=ns: e.dma_start(out=c.WoS[ns], in_=c.hw[ns]))
        p.inc_last("gpsimd", "dCast", 16)
    c.ev_cast_all = p.val("dCast")
    c.ev_cast_q = c.ev_cast_all
    c.ev_cast_s = c.ev_cast_all
    c.ev_cast_o = c.ev_cast_all
    c.ev_cast_h = c.ev_cast_all
    p.wait("vector", c.ev_consts)
    p.op("vector", lambda e: e.tensor_mul(c.s_wsu[:], c.s_wsuraw[:], c.s_sudp[:]))
    p.op("vector", lambda e: e.memset(c.car_kv[:], 0.0))
    ev_carz = p.opi("vector", lambda e: e.memset(c.car_w[:], 0.0), "sV")
    c.ev_carry_upd = [ev_carz] * NG

    ev_b7 = None
    for cc in range(16):
        ev_rs = p.opi("vector", lambda e, cc=cc: e.tensor_scalar(
            c.pre_misc[:], c.s_h128[:], c.s_suug[:, cc:cc + 1], None, op0=OP.mult), "sV")
        p.wait("tensor", ev_rs)
        p.wait("tensor", c.ev_consts)
        p.wait("tensor", ev_b7)
        ev_tp = p.opi("tensor", lambda e: e.transpose(
            c.pbb[7][:, 0:128], c.pre_misc[:], c.s_iden[:]), "sT")
        p.wait("scalar", ev_tp)
        ev_b7 = p.opi("scalar", lambda e, cc=cc: e.activation(
            c.hsu[:, cc, :], c.pbb[7][:, 0:128], AF.Copy), "sS")
        p.wait("vector", ev_tp)

    st_c = {"prein": {0: None, 1: None}, "preout": {0: None, 1: None},
            "bank": {0: None, 1: None, 4: None, 5: None}}
    st_w = {"prein": {0: None, 1: None}, "preout": {0: None, 1: None},
            "bank": {2: None, 3: None}}
    c_in = {0: c.pre_in0, 1: c.pre_in0b}
    c_out = {0: c.pre_out0, 1: c.pre_out0b}
    w_in = {0: c.pre_in1, 1: c.pre_in1b}
    w_out = {0: c.pre_out1, 1: c.pre_out1b}

    def cross_step(host_ap, lhsT, pblk, msl, t1_view):
        M = host_ap.shape[2]
        free = pblk * msl
        assert free == 8192
        nmm = free // 512
        last_store = None
        for sl in range(M // msl):
            h = sl % 2
            p.wait("gpsimd", st_c["prein"][h])
            srcv = host_ap[:, :, sl * msl:(sl + 1) * msl]
            p.op("gpsimd", lambda e, s=srcv, h=h: e.dma_start(
                out=c_in[h][:].rearrange("q (b m) -> q b m", b=pblk), in_=s))
            ev_load = p.inc_last("gpsimd", f"dPc{h}", 16)
            p.wait("tensor", ev_load)
            p.wait("tensor", st_c["preout"][h])
            for j in range(nmm):
                bk = j % 2 + 4 * h
                p.wait("tensor", st_c["bank"][bk])
                ev_mm = p.opi("tensor", lambda e, j=j, b=bk, h=h: e.matmul(
                    c.pb[b][:], lhsT, c_in[h][:, j * 512:(j + 1) * 512],
                    start=True, stop=True), "sT")
                p.wait("scalar", ev_mm)
                ev_ex = p.opi("scalar", lambda e, j=j, b=bk, h=h: e.activation(
                    c_out[h][:, j * 512:(j + 1) * 512], c.pb[b][:], AF.Copy), "sS")
                st_c["bank"][bk] = ev_ex
            st_c["prein"][h] = ev_mm
            p.wait("sync", ev_ex)
            npe = t1_view.shape[1]
            ncc = t1_view.shape[0]
            for ccx in range(ncc):
                dst = t1_view[ccx, :, :, sl * msl:(sl + 1) * msl]
                sv = c_out[h][ccx * npe:(ccx + 1) * npe, :].rearrange(
                    "q (b m) -> q b m", b=pblk)
                p.op("sync", lambda e, d=dst, s=sv: e.dma_start(out=d, in_=s))
                last_store = p.inc_last("sync", f"dPs{h}", 16)
            st_c["preout"][h] = last_store
        return [p.val("dPs0"), p.val("dPs1")]

    def within_step(t1_k, lhsT_for, scale, dest_fn, nkc, M, ev_t1_done):
        nmm = M // 512
        for cc in range(nkc):
            h = cc % 2
            p.wait("sync", ev_t1_done)
            p.wait("sync", st_w["prein"][h])
            srcv = t1_k[cc]
            p.op("sync", lambda e, s=srcv, M=M, h=h: e.dma_start(
                out=w_in[h][:, 0:M], in_=s))
            ev_load = p.inc_last("sync", f"dPl{h}", 16)
            p.wait("tensor", ev_load)
            p.wait("tensor", st_w["preout"][h])
            for j in range(nmm):
                bk = 2 + j % 2
                p.wait("tensor", st_w["bank"][bk])
                ev_mm = p.opi("tensor", lambda e, j=j, b=bk, cc=cc, h=h: e.matmul(
                    c.pb[b][:], lhsT_for(cc), w_in[h][:, j * 512:(j + 1) * 512],
                    start=True, stop=True), "sT")
                p.wait("scalar", ev_mm)
                ev_ex = p.opi("scalar", lambda e, j=j, b=bk, scale=scale, h=h: e.activation(
                    w_out[h][:, j * 512:(j + 1) * 512], c.pb[b][:], AF.Copy,
                    scale=scale), "sS")
                st_w["bank"][bk] = ev_ex
            st_w["prein"][h] = ev_mm
            p.wait("sync", ev_ex)
            dst, sv = dest_fn(cc, h)
            p.op("sync", lambda e, d=dst, s=sv: e.dma_start(out=d, in_=s))
            st_w["preout"][h] = p.inc_last("sync", f"dPo{h}", 16)

    p.wait("tensor", c.ev_consts)
    ev_hsu_done = p.val("sS")

    t1v = {}
    for nm, t1 in (("u", c.T1u), ("g", c.T1g)):
        t1v[nm] = t1.rearrange("(cc pb pe) m -> cc pe pb m", cc=16, pb=16, pe=8)
    t1v["d"] = c.T1d.rearrange("(cc pb pe) m -> cc pe pb m", cc=64, pb=64, pe=2)

    ev_t1 = {}
    ev_t1["u"] = cross_step(c.hu, c.s_h16i8[:], 16, 512, t1v["u"])

    def mk_dest_ug(hslot):
        def dest(cc, h):
            d = c.WugS[:, :, hslot, cc, :].rearrange("m q mm -> q m mm")
            s = w_out[h][:, 0:DFF].rearrange("q (m mm) -> q m mm", m=64)
            return d, s
        return dest

    def mk_tk(t1, nkc):
        return [t1[i * 128:(i + 1) * 128, :] for i in range(nkc)]

    within_step(mk_tk(c.T1u, 16), lambda cc: c.hsu[:, cc, :], RS2048,
                mk_dest_ug(0), 16, DFF, ev_t1["u"])
    ev_t1["g"] = cross_step(c.hg, c.s_h16i8[:], 16, 512, t1v["g"])
    within_step(mk_tk(c.T1g, 16), lambda cc: c.hsu[:, cc, :], RS2048,
                mk_dest_ug(1), 16, DFF, ev_t1["g"])
    ev_t1["d"] = cross_step(c.hd, c.s_h64i2[:], 64, 128, t1v["d"])

    def dest_dn(cc, h):
        d = c.WdS[:, cc // 8, :, cc % 8, :].rearrange("ns q mm -> q ns mm")
        s = w_out[h][:, 0:D].rearrange("q (ns mm) -> q ns mm", ns=4)
        return d, s
    within_step(mk_tk(c.T1d, 64), lambda cc: c.s_h128[:], RS8192,
                dest_dn, 64, D, ev_t1["d"])

    for bk, ev in list(st_c["bank"].items()) + list(st_w["bank"].items()):
        c.ev_bank0_pre = None
    if "t1u" in c.taps:
        p.wait("gpsimd", ev_t1["u"])
        p.op("gpsimd", lambda e: e.dma_start(out=c.taps["t1u"], in_=c.T1u))
        p.inc_last("gpsimd", "dTap", 16)
    if "wug" in c.taps:
        p.wait("gpsimd", p.val("dS"))
        p.op("gpsimd", lambda e: e.dma_start(out=c.taps["wug"], in_=c.WugS))
        p.inc_last("gpsimd", "dTap", 16)
    c.ev_pre_t = p.val("sT")
    c.ev_pre_s = p.val("sS")
    c.ev_pre_sync = [p.val("dPs0"), p.val("dPs1"), p.val("dPo0"), p.val("dPo1"),
                     p.val("dPl0"), p.val("dPl1")]


def emit_tile(c, it):
    p = c.prog
    t0 = it * T
    bank = c.ev_bank

    p.wait("gpsimd", c.ev_r2_free)
    p.op("gpsimd", lambda e: e.dma_start(
        out=c.x_fm_s[:],
        in_=c.x_fm[:, t0:t0 + T].rearrange("(kc q) t -> q kc t", q=128)))
    ev_x = p.inc_last("gpsimd", "dX", 16)

    p.wait("scalar", ev_x)
    ev_conv = {0: None, 1: None}
    for cc in range(16):
        r = cc % 2
        p.wait("scalar", c.ev_lrot_free[r])
        p.op("scalar", lambda e, cc=cc, r=r: e.activation(
            c.xc_rot[:, r, :], c.x_fm_s[:, cc, :], AF.Copy))
        ev_conv[r] = p.opi("scalar", lambda e, cc=cc, r=r: e.activation(
            c.xs_rot[:, r, :], c.x_fm_s[:, cc, :], AF.Square), "sS")
        p.wait("tensor", ev_conv[r])
        if cc == 0:
            p.wait("tensor", bank[0])
            p.wait("tensor", bank[1])
        p.op("tensor", lambda e, r=r, cc=cc: e.matmul(
            c.pb[0][:], c.s_onesq[:], c.xc_rot[:, r, :],
            start=(cc == 0), stop=(cc == 15), skip_group_check=True))
        p.op("tensor", lambda e, r=r, cc=cc: e.matmul(
            c.pb[1][:], c.s_onesq[:], c.xs_rot[:, r, :],
            start=(cc == 0), stop=(cc == 15), skip_group_check=True))
        ev_cc = p.inc_last("tensor", "sT")
        c.ev_lrot_free[r] = ev_cc
    ev_stats = ev_cc

    p.wait("vector", ev_stats)
    p.wait("vector", c.ev_ltiles_free)
    p.op("vector", lambda e: e.tensor_scalar(c.bm[:], c.pb[0][:], 1.0 / D, None, op0=OP.mult))
    p.op("vector", lambda e: e.tensor_mul(c.l_t1[:], c.bm[:], c.bm[:]))
    ev_var = p.opi("vector", lambda e: e.scalar_tensor_tensor(
        c.l_t2[:], c.pb[1][:], 1.0 / D, c.l_t1[:], op0=OP.mult, op1=OP.subtract), "sV")
    bank[0] = ev_var
    bank[1] = ev_var
    p.wait("scalar", ev_var)
    p.wait("scalar", c.ev_consts)
    ev_sd = p.opi("scalar", lambda e: e.activation(
        c.l_t1[:], c.l_t2[:], AF.Sqrt, bias=EPS_LN), "sS")
    p.wait("vector", ev_sd)
    ev_bs = p.opi("vector", lambda e: e.reciprocal(c.bs[:], c.l_t1[:]), "sV")
    p.op("vector", lambda e: e.tensor_mul(c.l_t3[:], c.bs[:], c.bs[:]))
    ev_r2 = p.opi("vector", lambda e: e.tensor_mul(c.l_t4[:], c.l_t3[:], c.l_t2[:]), "sV")
    p.wait("scalar", ev_r2)
    ev_sf = p.opi("scalar", lambda e: e.activation(
        c.l_t3[:], c.l_t4[:], AF.Sqrt, bias=EPS_RMS), "sS")
    p.wait("vector", ev_sf)
    p.op("vector", lambda e: e.reciprocal(c.f1t[:], c.l_t3[:]))
    p.op("vector", lambda e: e.tensor_scalar(c.s1pt[:], c.bs[:], 1.0, None, op0=OP.add))
    ev_tiles = p.opi("vector", lambda e: e.scalar_tensor_tensor(
        c.msnt[:], c.bm[:], -1.0, c.bs[:], op0=OP.mult, op1=OP.mult), "sV")
    p.wait("tensor", ev_tiles)
    ev_f1 = None
    for qi, (tile, colt) in enumerate(((c.f1t, c.col_f1), (c.s1pt, c.col_s1p),
                                       (c.msnt, c.col_msn))):
        bt = 2 + qi % 2
        p.wait("tensor", bank[bt])
        for tb in range(TBN):
            p.op("tensor", lambda e, tile=tile, tb=tb, bt=bt: e.transpose(
                c.pb[bt][:, tb * 128:(tb + 1) * 128],
                tile[:, tb * 128:(tb + 1) * 128], c.s_idenf[:]))
        ev_tp = p.inc_last("tensor", "sT")
        p.wait("scalar", ev_tp)
        for tb in range(TBN):
            p.op("scalar", lambda e, colt=colt, tb=tb, bt=bt: e.activation(
                colt[:, tb:tb + 1], c.pb[bt][:, tb * 128:tb * 128 + 1], AF.Copy))
        ev_f1 = p.inc_last("scalar", "sS")
        bank[bt] = ev_f1
    ev_s1p = ev_f1
    c.ev_ltiles_free = None
    if "cols" in c.taps:
        p.wait("vector", ev_f1)
        p.op("vector", lambda e: e.tensor_copy(c.col_dbg[:, 2, :], c.col_f1[:]))
        p.op("vector", lambda e: e.tensor_copy(c.col_dbg[:, 3, :], c.col_s1p[:]))
        ev_dbg = p.opi("vector", lambda e: e.tensor_copy(
            c.col_dbg[:, 4, :], c.col_msn[:]), "sV")
        p.wait("sync", ev_dbg)
        p.op("sync", lambda e, it=it: e.dma_start(
            out=c.taps["cols"][it], in_=c.col_dbg[:]))
        p.inc_last("sync", "dTap", 16)

    p.wait("vector", ev_bs)
    p.wait("vector", c.ev_xn_free)
    bmb = c.bm[:].unsqueeze(1).broadcast_to([128, 16, T])
    bsb = c.bs[:].unsqueeze(1).broadcast_to([128, 16, T])
    p.op("vector", lambda e, bmb=bmb: e.tensor_tensor(
        c.x_fm_s[:], c.x_fm_s[:], bmb, op=OP.subtract))
    ev_xn = p.opi("vector", lambda e, bsb=bsb: e.tensor_tensor(
        c.xn_bf[:], c.x_fm_s[:], bsb, op=OP.mult), "sV")
    c.ev_ltiles_free = ev_xn
    if "xn" in c.taps:
        p.wait("sync", ev_xn)
        p.op("sync", lambda e: e.dma_start(
            out=c.taps["xn"][:, t0:t0 + T].rearrange("(kc q) t -> q kc t", q=128),
            in_=c.xn_bf[:]))
        p.inc_last("sync", "dTap", 16)

    ev_q_ns = {}
    for ns in range(12):
        bank0 = (ns % 2) * 4
        for half in range(2):
            buf = half
            p.wait("gpsimd", c.ev_war_free[buf])
            src = c.hq[ns, :, half * 8:(half + 1) * 8, :]
            p.op("gpsimd", lambda e, s=src, b=buf: e.dma_start(
                out=c.warena[:, b].rearrange("q (kc m) -> q kc m", kc=8), in_=s))
            ev_w = p.inc_last("gpsimd", f"dW{buf}", 16)
            p.wait("tensor", ev_w)
            if ns == 0 and half == 0:
                p.wait("tensor", ev_xn)
            for tb in range(TBN):
                bk = bank0 + tb
                if half == 0:
                    p.wait("tensor", bank[bk])
                for kc8 in range(8):
                    kc = half * 8 + kc8
                    p.op("tensor", lambda e, tb=tb, kc=kc, kc8=kc8, b=buf, bk=bk, half=half: e.matmul(
                        c.pb[bk][:],
                        c.xn_bf[:, kc, tb * 128:(tb + 1) * 128],
                        c.warena[:, b].rearrange("q (kc m) -> q kc m", kc=8)[:, kc8, :],
                        start=(half == 0 and kc8 == 0), stop=(half == 1 and kc8 == 7)))
            ev_half = p.inc_last("tensor", "sT")
            c.ev_war_free[buf] = ev_half
        ev_ns = ev_half
        p.wait("scalar", ev_ns)
        p.wait("scalar", ev_f1)
        if ns == 0:
            p.wait("scalar", c.ev_qkvbuf_free)
        for tb in range(TBN):
            bk = bank0 + tb
            p.op("scalar", lambda e, ns=ns, tb=tb, bk=bk: e.activation(
                c.qkv_bf[:, tb, ns * 512:(ns + 1) * 512], c.pb[bk][:], AF.Copy,
                scale=c.col_f1[:, tb:tb + 1]))
            bank[bk] = p.inc_last("scalar", "sS")
        ev_q_ns[ns] = bank[bank0 + 3]
    if "qkv" in c.taps:
        p.wait("sync", p.val("sS"))
        p.op("sync", lambda e: e.dma_start(
            out=c.taps["qkv"][t0:t0 + T, :].rearrange("(tb q) m -> q tb m", q=128),
            in_=c.qkv_bf[:]))
        p.inc_last("sync", "dTap", 16)

    p.wait("scalar", ev_q_ns[7])
    p.wait("scalar", c.ev_r5_free)
    for tb in range(TBN):
        p.op("scalar", lambda e, tb=tb: e.activation(
            c.qsq[:], c.qkv_bf[:, tb, 0:D], AF.Square,
            accum_out=c.col_sx[:, tb:tb + 1]))
        p.op("scalar", lambda e, tb=tb: e.activation(
            c.qsq[:], c.qkv_bf[:, tb, D:2 * D], AF.Square,
            accum_out=c.col_sx2[:, tb:tb + 1]))
    ev_sq = p.inc_last("scalar", "sS")
    p.op("scalar", lambda e: e.activation(
        c.col_tq[:], c.col_sx[:], AF.Sqrt, scale=1.0 / D, bias=EPS_RMS))
    ev_sk1 = p.opi("scalar", lambda e: e.activation(
        c.col_tk[:], c.col_sx2[:], AF.Sqrt, scale=1.0 / D, bias=EPS_RMS), "sS")
    p.wait("vector", ev_sk1)
    p.op("vector", lambda e: e.reciprocal(c.col_sq[:], c.col_tq[:]))
    ev_qkcols_v = p.opi("vector", lambda e: e.reciprocal(c.col_sk[:], c.col_tk[:]), "sV")
    ev_qkcols = ev_qkcols_v

    ev_T_pe_last = None
    for g in range(NG):
        gr = g % 2
        p.wait("scalar", ev_qkcols)
        p.wait("scalar", ev_q_ns[8 + g])
        p.wait("scalar", c.ev_wg_free[gr])
        for tb in range(TBN):
            p.op("scalar", lambda e, g=g, tb=tb, gr=gr: e.activation(
                c.wg_w[:, gr, tb, :], c.qkv_bf[:, tb, D + g * 512:D + (g + 1) * 512],
                AF.Exp, scale=c.col_sk[:, tb:tb + 1]))
        ev_w = p.inc_last("scalar", "sS")
        p.wait("vector", ev_w)
        vsl = c.qkv_bf[:].transpose([0, 1, 2])[:, :, 2 * D + g * 512:2 * D + (g + 1) * 512]
        p.op("vector", lambda e, g=g, gr=gr, vsl=vsl: e.tensor_mul(
            c.wg_wv[:, gr], c.wg_w[:, gr], vsl))
        ev_wv = p.inc_last("vector", "sV")

        p.wait("tensor", ev_wv)
        p.wait("tensor", c.ev_carry_upd[g])
        bkv, bw = gr * 2, gr * 2 + 1
        for tb in range(TBN):
            rot = tb % 2
            p.wait("tensor", bank[bkv])
            p.wait("tensor", bank[bw])
            for bk, dat, car in ((bkv, c.wg_wv, c.car_kv), (bw, c.wg_w, c.car_w)):
                p.op("tensor", lambda e, bk=bk, d=dat, tb=tb, gr=gr: e.matmul(
                    c.pb[bk][:], c.s_utri[:], d[:, gr, tb, :], start=True, stop=False))
                for tbp in range(tb):
                    p.op("tensor", lambda e, bk=bk, d=dat, tbp=tbp, gr=gr: e.matmul(
                        c.pb[bk][:], c.s_onesq[:], d[:, gr, tbp, :],
                        start=False, stop=False))
                p.op("tensor", lambda e, bk=bk, car=car, g=g: e.matmul(
                    c.pb[bk][:], c.s_onesr[:], car[0:1, g * 512:(g + 1) * 512],
                    start=False, stop=True))
            ev_cs = p.inc_last("tensor", "sT")
            if tb == 0:
                p.wait("tensor", bank[4])
                p.wait("tensor", bank[5])
            p.op("tensor", lambda e, tb=tb, gr=gr: e.matmul(
                c.pb[4][:], c.s_onesq[:], c.wg_wv[:, gr, tb, :],
                start=(tb == 0), stop=(tb == 3)))
            p.op("tensor", lambda e, tb=tb, gr=gr: e.matmul(
                c.pb[5][:], c.s_onesq[:], c.wg_w[:, gr, tb, :],
                start=(tb == 0), stop=(tb == 3)))
            if tb == 3:
                ev_tot = p.inc_last("tensor", "sT")

            p.wait("scalar", ev_cs)
            p.wait("scalar", c.ev_trot_free[rot])
            p.op("scalar", lambda e, bw=bw, rot=rot: e.activation(
                c.dn_rot[:, rot, :], c.pb[bw][:], AF.Copy, bias=EPS_RATIO))
            p.op("scalar", lambda e, g=g, tb=tb, rot=rot: e.activation(
                c.sig_rot[:, rot, :], c.qkv_bf[:, tb, g * 512:(g + 1) * 512],
                AF.Sigmoid, scale=c.col_sq[:, tb:tb + 1]))
            ev_dn = p.inc_last("scalar", "sS")
            p.wait("vector", ev_dn)
            p.wait("vector", c.ev_yrot_free[rot])
            p.op("vector", lambda e, rot=rot: e.reciprocal(
                c.rec_rot[:, rot, :], c.dn_rot[:, rot, :]))
            p.op("vector", lambda e, bkv=bkv, rot=rot: e.tensor_mul(
                c.t2_rot[:, rot, :], c.pb[bkv][:], c.rec_rot[:, rot, :]))
            p.op("vector", lambda e, rot=rot: e.tensor_mul(
                c.y_rot[:, rot, :], c.t2_rot[:, rot, :], c.sig_rot[:, rot, :]))
            ev_y = p.inc_last("vector", "sV")
            c.ev_trot_free[rot] = ev_y
            bank[bkv] = ev_y
            bank[bw] = ev_dn

            p.wait("tensor", ev_y)
            if tb == 0:
                p.wait("tensor", bank[6])
                p.wait("tensor", bank[7])
            for kcs in range(4):
                bt = 6 if kcs < 2 else 7
                colbase = (kcs % 2) * 512
                p.op("tensor", lambda e, rot=rot, kcs=kcs, tb=tb, bt=bt, colbase=colbase: e.transpose(
                    c.pbb[bt][:, colbase + tb * 128:colbase + (tb + 1) * 128],
                    c.y_rot[:, rot, kcs * 128:(kcs + 1) * 128], c.s_iden[:]))
            ev_tp = p.inc_last("tensor", "sT")
            ev_T_pe_last = ev_tp
            c.ev_yrot_free[rot] = ev_tp

        p.wait("scalar", ev_tp)
        p.wait("scalar", c.ev_r5_free)
        for kcs in range(4):
            bt = 6 if kcs < 2 else 7
            colbase = (kcs % 2) * 512
            p.op("scalar", lambda e, g=g, kcs=kcs, bt=bt, colbase=colbase: e.activation(
                c.y_fm[:, g * 4 + kcs, :], c.pbb[bt][:, colbase:colbase + 512], AF.Copy))
        ev_yevac = p.inc_last("scalar", "sS")
        bank[6] = ev_yevac
        bank[7] = ev_yevac
        p.wait("scalar", ev_tot)
        p.wait("scalar", c.ev_rowrot_free)
        ev_rk = p.opi("scalar", lambda e: e.activation(
            c.row_rot[:], c.pb[4][0:1, :], AF.Copy), "sS")
        p.wait("vector", ev_rk)
        ev_ck = p.opi("vector", lambda e, g=g: e.tensor_add(
            c.car_kv[0:1, g * 512:(g + 1) * 512],
            c.car_kv[0:1, g * 512:(g + 1) * 512], c.row_rot[:]), "sV")
        p.wait("scalar", ev_ck)
        ev_rw = p.opi("scalar", lambda e: e.activation(
            c.row_rot[:], c.pb[5][0:1, :], AF.Copy), "sS")
        p.wait("vector", ev_rw)
        ev_cw = p.opi("vector", lambda e, g=g: e.tensor_add(
            c.car_w[0:1, g * 512:(g + 1) * 512],
            c.car_w[0:1, g * 512:(g + 1) * 512], c.row_rot[:]), "sV")
        c.ev_carry_upd[g] = ev_cw
        c.ev_rowrot_free = ev_cw
        bank[4] = ev_rk
        bank[5] = ev_rw
        c.ev_wg_free[gr] = ev_tot

    c.ev_qkvbuf_free = [p.val("sS"), p.val("sV")]
    if "y" in c.taps:
        p.wait("sync", p.val("sS"))
        p.op("sync", lambda e: e.dma_start(
            out=c.taps["y"][:, t0:t0 + T].rearrange("(kc q) t -> q kc t", q=128),
            in_=c.y_fm[:]))
        p.inc_last("sync", "dTap", 16)

    ev_yfm_ready = p.val("sS")
    for m in range(64):
        buf = m % 2
        p.wait("gpsimd", c.ev_war_free[buf])
        for hs_, mall in ((0, m), (1, m + 64)):
            src = c.hs[mall]
            p.op("gpsimd", lambda e, s=src, b=buf, hs_=hs_: e.dma_start(
                out=c.warena[:, b].rearrange("q (h kc m) -> q h kc m", h=2, kc=16)[:, hs_],
                in_=s))
            ev_w = p.inc_last("gpsimd", f"dW{buf}", 16)
        p.wait("tensor", ev_w)
        if m == 0:
            p.wait("tensor", ev_yfm_ready)
            p.wait("tensor", c.ev_h_free)
        bu, bg = (0, 1) if m % 2 == 0 else (2, 3)
        p.wait("tensor", bank[bu])
        p.wait("tensor", bank[bg])
        wv4 = c.warena[:, buf].rearrange("q (h kc m) -> q h kc m", h=2, kc=16)
        for kc in range(16):
            p.op("tensor", lambda e, kc=kc, bu=bu, wv4=wv4: e.matmul(
                c.pb[bu][:], wv4[:, 0, kc, :], c.y_fm[:, kc, :],
                start=(kc == 0), stop=(kc == 15)))
        for kc in range(16):
            p.op("tensor", lambda e, kc=kc, bg=bg, wv4=wv4: e.matmul(
                c.pb[bg][:], wv4[:, 1, kc, :], c.y_fm[:, kc, :],
                start=(kc == 0), stop=(kc == 15)))
        ev_mm = p.inc_last("tensor", "sT")
        c.ev_war_free[buf] = ev_mm
        rot = m % 2
        p.wait("scalar", ev_mm)
        p.wait("scalar", c.ev_silu_free[rot])
        ev_si = p.opi("scalar", lambda e, bg=bg, rot=rot: e.activation(
            c.silu_rot[:, rot, :], c.pb[bg][:], AF.Silu), "sS")
        p.wait("vector", ev_si)
        ev_h = p.opi("vector", lambda e, m=m, bu=bu, rot=rot: e.scalar_tensor_tensor(
            c.h_fm[:, m, :], c.pb[bu][:], 1.0, c.silu_rot[:, rot, :],
            op0=OP.mult, op1=OP.mult), "sV")
        c.ev_silu_free[rot] = ev_h
        bank[bu] = ev_h
        bank[bg] = ev_si
    ev_h_all = p.val("sV")
    if "h" in c.taps:
        p.wait("sync", ev_h_all)
        p.op("sync", lambda e: e.dma_start(
            out=c.taps["h"][:, t0:t0 + T].rearrange("(kc q) t -> q kc t", q=128),
            in_=c.h_fm[:]))
        p.inc_last("sync", "dTap", 16)

    for tbp in range(2):
        tbs = (tbp * 2, tbp * 2 + 1)
        for ns in range(4):
            for kcg in range(8):
                buf = (ns * 8 + kcg) % 2
                if tbp == 0 and ns == 0 and kcg == 0:
                    p.wait("sync", c.ev_cast_o)
                p.wait("sync", c.ev_war_free[buf])
                src = c.WoS[ns, kcg]
                p.op("sync", lambda e, s=src, b=buf: e.dma_start(
                    out=c.warena[:, b].rearrange("q (kc m) -> q kc m", kc=8), in_=s))
                ev_w = p.inc_last("sync", f"dW{buf}", 16)
                p.wait("tensor", ev_w)
                if tbp == 0 and ns == 0 and kcg == 0:
                    p.wait("tensor", ev_h_all)
                for ti, tb in enumerate(tbs):
                    bk = ti * 4 + ns
                    if kcg == 0:
                        p.wait("tensor", bank[bk])
                    for kc in range(8):
                        p.op("tensor", lambda e, kcg=kcg, kc=kc, tb=tb, bk=bk, b=buf: e.matmul(
                            c.pb[bk][:],
                            c.h_fm[:, kcg * 8 + kc, tb * 128:(tb + 1) * 128],
                            c.warena[:, b].rearrange("q (kc m) -> q kc m", kc=8)[:, kc, :],
                            start=(kcg == 0 and kc == 0), stop=(kcg == 7 and kc == 7)))
                ev_mm = p.inc_last("tensor", "sT")
                c.ev_war_free[buf] = ev_mm
        p.wait("scalar", ev_mm)
        if tbp == 0:
            p.wait("scalar", c.ev_r5_free_mlp)
        for ti, tb in enumerate(tbs):
            for ns in range(4):
                bk = ti * 4 + ns
                p.op("scalar", lambda e, tb=tb, ns=ns, bk=bk: e.activation(
                    c.mlp_bf[:, tb, ns * 512:(ns + 1) * 512], c.pb[bk][:], AF.Copy))
                bank[bk] = p.inc_last("scalar", "sS")
    ev_mlp = p.val("sS")
    c.ev_h_free = p.val("sT")
    if "mlp" in c.taps:
        p.wait("sync", ev_mlp)
        p.op("sync", lambda e: e.dma_start(
            out=c.taps["mlp"][t0:t0 + T, :].rearrange("(tb q) m -> q tb m", q=128),
            in_=c.mlp_bf[:]))
        p.inc_last("sync", "dTap", 16)

    for m in range(64):
        buf = m % 2
        if m == 0:
            p.wait("sync", c.ev_pre_sync)
        p.wait("sync", c.ev_war_free[buf])
        src = c.WugS[m].rearrange("q h kc mm -> q (h kc mm)")
        p.op("sync", lambda e, s=src, b=buf: e.dma_start(
            out=c.warena[:, b], in_=s))
        ev_w = p.inc_last("sync", f"dW{buf}", 16)
        p.wait("tensor", ev_w)
        if m == 0:
            p.wait("tensor", c.ev_f_free)
        wv4 = c.warena[:, buf].rearrange("q (h kc m) -> q h kc m", h=2, kc=16)
        bu, bg = (0, 1) if m % 2 == 0 else (2, 3)
        p.wait("tensor", bank[bu])
        p.wait("tensor", bank[bg])
        for kc in range(16):
            p.op("tensor", lambda e, kc=kc, bu=bu, wv4=wv4: e.matmul(
                c.pb[bu][:], wv4[:, 0, kc, :], c.xn_bf[:, kc, :],
                start=(kc == 0), stop=(kc == 15)))
        for kc in range(16):
            p.op("tensor", lambda e, kc=kc, bg=bg, wv4=wv4: e.matmul(
                c.pb[bg][:], wv4[:, 1, kc, :], c.xn_bf[:, kc, :],
                start=(kc == 0), stop=(kc == 15)))
        ev_mm = p.inc_last("tensor", "sT")
        c.ev_war_free[buf] = ev_mm
        rot = m % 2
        p.wait("scalar", ev_mm)
        p.wait("scalar", c.ev_silu_free[rot])
        ev_si = p.opi("scalar", lambda e, bg=bg, rot=rot, m=m: e.activation(
            c.silu_rot[:, rot, :], c.pb[bg][:], AF.Silu,
            scale=c.s_wsg[:, m:m + 1]), "sS")
        p.wait("vector", ev_si)
        ev_f = p.opi("vector", lambda e, m=m, bu=bu, rot=rot: e.scalar_tensor_tensor(
            c.f_fm[:, m, :], c.pb[bu][:], c.s_wsu[:, m:m + 1],
            c.silu_rot[:, rot, :], op0=OP.mult, op1=OP.mult), "sV")
        c.ev_silu_free[rot] = ev_f
        bank[bu] = ev_f
        bank[bg] = ev_si
    ev_f_all = p.val("sV")
    c.ev_xn_free = p.val("sT")
    if "f" in c.taps:
        p.wait("sync", ev_f_all)
        p.op("sync", lambda e: e.dma_start(
            out=c.taps["f"][:, t0:t0 + T].rearrange("(kc q) t -> q kc t", q=128),
            in_=c.f_fm[:]))
        p.inc_last("sync", "dTap", 16)

    p.wait("tensor", c.ev_h_free)
    for j in range(4):
        p.wait("tensor", bank[7])
        ev_b = p.opi("tensor", lambda e, j=j: e.matmul(
            c.pb[7][:], c.s_onesrf[:], c.s_wsdr[0:1, j * 512:(j + 1) * 512],
            start=True, stop=True), "sT")
        p.wait("scalar", ev_b)
        p.op("scalar", lambda e, j=j: e.activation(
            c.wsd_tile[:, j * 512:(j + 1) * 512], c.pb[7][:], AF.Copy))
        bank[7] = p.inc_last("scalar", "sS")
    ev_wsdt = p.val("sS")

    for tbp in range(2):
        tbs = (tbp * 2, tbp * 2 + 1)
        for ti, tb in enumerate(tbs):
            rot = tb % 2
            p.wait("gpsimd", c.ev_xtm_free[rot])
            p.wait("gpsimd", c.ev_h_free)
            p.op("gpsimd", lambda e, tb=tb, rot=rot: e.dma_start(
                out=c.x_tm_rot[:, rot, :],
                in_=c.x_tm[t0 + tb * 128:t0 + (tb + 1) * 128, :]))
            c.ev_xtm_load[rot] = p.inc_last("gpsimd", f"dX{rot}", 16)
        for ns in range(4):
            for kcg in range(8):
                buf = (ns * 8 + kcg) % 2
                p.wait("sync", c.ev_pre_sync)
                p.wait("sync", c.ev_war_free[buf])
                src = c.WdS[ns, kcg].rearrange("q kc mm -> q (kc mm)")
                p.op("sync", lambda e, s=src, b=buf: e.dma_start(
                    out=c.warena[:, b], in_=s))
                ev_w = p.inc_last("sync", f"dW{buf}", 16)
                p.wait("tensor", ev_w)
                for ti, tb in enumerate(tbs):
                    bk = ti * 4 + ns
                    if kcg == 0:
                        p.wait("tensor", bank[bk])
                    for kc in range(8):
                        p.op("tensor", lambda e, kcg=kcg, kc=kc, tb=tb, bk=bk, b=buf: e.matmul(
                            c.pb[bk][:],
                            c.f_fm[:, kcg * 8 + kc, tb * 128:(tb + 1) * 128],
                            c.warena[:, b].rearrange("q (kc m) -> q kc m", kc=8)[:, kc, :],
                            start=(kcg == 0 and kc == 0), stop=(kcg == 7 and kc == 7)))
                ev_mm = p.inc_last("tensor", "sT")
                c.ev_war_free[buf] = ev_mm
        for ti, tb in enumerate(tbs):
            rot = tb % 2
            p.wait("vector", ev_mm)
            p.wait("vector", ev_wsdt)
            p.wait("vector", c.ev_xtm_load[rot])
            p.wait("vector", ev_mlp)
            p.wait("vector", c.ev_oa_free)
            p.op("vector", lambda e, rot=rot, tb=tb: e.tensor_scalar(
                c.oa_a[:], c.x_tm_rot[:, rot, :], c.col_s1p[:, tb:tb + 1],
                c.col_msn[:, tb:tb + 1], op0=OP.mult, op1=OP.add))
            p.op("vector", lambda e, tb=tb: e.tensor_add(
                c.oa_b[:], c.oa_a[:], c.mlp_bf[:, tb, :]))
            for ns in range(4):
                bk = ti * 4 + ns
                p.op("vector", lambda e, bk=bk, ns=ns: e.tensor_mul(
                    c.oa_a[:, ns * 512:(ns + 1) * 512], c.pb[bk][:],
                    c.wsd_tile[:, ns * 512:(ns + 1) * 512]))
            ev_out = p.opi("vector", lambda e: e.tensor_add(
                c.oa_a[:], c.oa_a[:], c.oa_b[:]), "sV")
            for ns in range(4):
                bank[ti * 4 + ns] = ev_out
            c.ev_xtm_free[rot] = ev_out
            p.wait("gpsimd", ev_out)
            p.op("gpsimd", lambda e, tb=tb: e.dma_start(
                out=c.out[t0 + tb * 128:t0 + (tb + 1) * 128, :], in_=c.oa_a[:]))
            c.ev_oa_free = p.inc_last("gpsimd", "dO", 16)

    c.ev_f_free = p.val("sT")
    c.ev_r2_free = c.ev_f_free
    c.ev_r5_free = p.val("sV")
    c.ev_r5_free_mlp = p.val("sT")


def emit_program(c):
    p = c.prog
    c.ev_bank = {i: None for i in range(8)}
    c.ev_r2_free = None
    c.ev_lrot_free = {0: None, 1: None}
    c.ev_xn_free = None
    c.ev_qkvbuf_free = None
    c.ev_war_free = {0: None, 1: None}
    c.ev_wg_free = {0: None, 1: None}
    c.ev_trot_free = {0: None, 1: None}
    c.ev_yrot_free = {0: None, 1: None}
    c.ev_rowrot_free = None
    c.ev_silu_free = {0: None, 1: None}
    c.ev_h_free = None
    c.ev_f_free = None
    c.ev_oa_free = None
    c.ev_xtm_free = {0: None, 1: None}
    c.ev_xtm_load = {0: None, 1: None}
    c.ev_r5_free = None
    c.ev_r5_free_mlp = None
    c.ev_ltiles_free = None

    emit_preamble(c)
    c.ev_r2_free = c.ev_pre_sync
    c.ev_qkvbuf_free = [c.ev_pre_s, c.ev_pre_sync, c.ev_pre_t]
    c.ev_h_free = c.ev_pre_t
    c.ev_f_free = c.ev_pre_t
    for i in range(8):
        c.ev_bank[i] = c.ev_pre_s

    for it in range(NT):
        emit_tile(c, it)

    for s in ("dK", "dPs0", "dPs1", "dPl0", "dPl1", "dPo0", "dPo1",
              "dW0", "dW1", "dTap"):
        p.wait("sync", p.val(s))
    for s in ("dCast", "dPc0", "dPc1", "dX", "dX0", "dX1", "dO", "dTap"):
        p.wait("gpsimd", p.val(s))


def replay(c):
    nc = c.nc
    es = contextlib.ExitStack()
    sem_names = set()
    for e in Prog.ENGS:
        for item in c.prog.q[e]:
            if item[0] == "w":
                sem_names.add(item[1])
            elif item[2] is not None:
                sem_names.add(item[2])
    sems = {n: es.enter_context(nc.semaphore(n)) for n in sorted(sem_names)}
    block = es.enter_context(nc.Block())

    def emit(eng, ename):
        for item in c.prog.q[ename]:
            if item[0] == "w":
                eng.wait_ge(sems[item[1]], item[2])
            else:
                _, fn, inc, inc_n = item
                ins = fn(eng)
                if inc is not None:
                    ins.then_inc(sems[inc], inc_n)

    @block.sync
    def _(e):
        emit(e, "sync")

    @block.tensor
    def _(e):
        emit(e, "tensor")

    @block.vector
    def _(e):
        emit(e, "vector")

    @block.scalar
    def _(e):
        emit(e, "scalar")

    @block.gpsimd
    def _(e):
        emit(e, "gpsimd")

    es.close()


def build_program():
    c = build_nc()
    emit_program(c)
    replay(c)
    return c


def kernel(**inputs):
    shared = _host_shared(inputs)
    x = np.asarray(inputs["L0_attn_x"], dtype=np.float32)
    in_maps = []
    for b in range(NCORES):
        m = dict(shared)
        m["x_fm"] = np.ascontiguousarray(x[b].T)
        m["x_tm"] = np.ascontiguousarray(x[b])
        in_maps.append(m)
    c = build_program()
    res = run_bass_kernel_spmd(c.nc, in_maps, core_ids=list(range(NCORES)))
    out = np.stack([np.asarray(r["out"]) for r in res.results], axis=0)
    return out.astype(np.float32)


# version 06
# speedup vs baseline: 1.1644x; 1.1644x over previous
import contextlib
import numpy as np
import ml_dtypes
import concourse.bass as bass
import concourse.mybir as mybir
from concourse.bass_utils import run_bass_kernel_spmd

F32 = mybir.dt.float32
BF16 = mybir.dt.bfloat16
AF = mybir.ActivationFunctionType
OP = mybir.AluOpType
BF = ml_dtypes.bfloat16

NCORES = 8
S = 4096
D = 2048
DQKV = 6144
DFF = 8192
T = 512
NT = S // T
TBN = T // 128
NG = 4
EPS_RMS = float(np.float32(1.1920929e-07))
EPS_LN = 1e-5
EPS_RATIO = 1e-6
RS2048 = float(1.0 / np.sqrt(np.float64(2048.0)))
RS8192 = float(1.0 / np.sqrt(np.float64(8192.0)))

DEBUG_TAPS = ()
SINGLE_PRE = False
SINGLE_CROSS = True
DETECT_RACES = True
NOCARRY = False
SINGLE_WITHIN = False


def _hadamard(n):
    h = np.array([[1.0]], dtype=np.float64)
    while h.shape[0] < n:
        h = np.block([[h, h], [h, -h]])
    return h


def _host_shared(inputs):
    d = {}
    Wqkv = np.ascontiguousarray(inputs["L0_attn_w_qkv"], dtype=np.float32)
    Wsw = np.ascontiguousarray(inputs["L0_attn_w_swiglu"], dtype=np.float32)
    Wout = np.ascontiguousarray(inputs["L0_attn_w_out"], dtype=np.float32)
    Wup = np.ascontiguousarray(inputs["L0_ffn_W_up"], dtype=np.float32)
    Wgate = np.ascontiguousarray(inputs["L0_ffn_W_gate"], dtype=np.float32)
    Wdown = np.ascontiguousarray(inputs["L0_ffn_W_down"], dtype=np.float32)

    d["hq"] = np.ascontiguousarray(
        Wqkv.reshape(12, 512, 16, 128).transpose(0, 3, 2, 1))
    d["hs"] = np.ascontiguousarray(
        Wsw.reshape(128, 128, 16, 128).transpose(0, 3, 2, 1))
    d["hw"] = np.ascontiguousarray(
        Wout.reshape(4, 512, 8, 8, 128).transpose(0, 2, 4, 3, 1))
    for nm, W in (("hu", Wup), ("hg", Wgate)):
        d[nm] = np.ascontiguousarray(
            W.T.reshape(16, 16, 8, DFF).transpose(0, 2, 1, 3).reshape(128, 16, DFF))
    d["hd"] = np.ascontiguousarray(
        Wdown.T.reshape(64, 64, 2, D).transpose(0, 2, 1, 3).reshape(128, 64, D))

    SU_ug = np.asarray(inputs["L0_ffn_SU_ug"], np.float32)
    SU_dp = np.asarray(inputs["L0_ffn_SU_dp"], np.float32)
    Wsc_ug = np.asarray(inputs["L0_ffn_Wscale_ug"], np.float32)
    Wsc_dp = np.asarray(inputs["L0_ffn_Wscale_dp"], np.float32)
    d["su_ug"] = np.ascontiguousarray(SU_ug.reshape(16, 128).T)
    d["su_dp"] = np.ascontiguousarray(SU_dp.reshape(64, 128).T)
    d["wsu_raw"] = np.ascontiguousarray(Wsc_ug[:DFF].reshape(64, 128).T)
    d["wsg"] = np.ascontiguousarray(Wsc_ug[DFF:].reshape(64, 128).T)
    d["wsd_row"] = np.ascontiguousarray(Wsc_dp.reshape(1, D))

    bfc = lambda a: np.ascontiguousarray(np.asarray(a, np.float32)).astype(BF)
    d["iden"] = bfc(np.eye(128))
    d["idenf"] = np.ascontiguousarray(np.eye(128, dtype=np.float32))
    d["constv"] = np.ascontiguousarray(np.broadcast_to(np.array(
        [[1e-5, float(np.float32(1.1920929e-07)), 1.0 / 2048.0, 1.0]],
        np.float32), (128, 4)).copy())
    d["onesq"] = bfc(np.ones((128, 128)))
    d["ones_col"] = bfc(np.ones((128, 1)))
    d["ones_row"] = bfc(np.ones((1, 128)))
    d["ones_row_f"] = np.ones((1, 128), np.float32)
    d["utri"] = bfc(np.triu(np.ones((128, 128))))
    d["h128"] = bfc(_hadamard(128))
    d["h16i8"] = bfc(np.kron(_hadamard(16), np.eye(8)))
    d["h64i2"] = bfc(np.kron(_hadamard(64), np.eye(2)))
    return d



class Prog:
    ENGS = ("sync", "tensor", "vector", "scalar", "gpsimd")

    def __init__(self):
        self.q = {e: [] for e in self.ENGS}
        self.cnt = {}
        self.waited = {e: {} for e in self.ENGS}

    def wait(self, eng, ev):
        if ev is None:
            return
        if isinstance(ev, list):
            for e in ev:
                self.wait(eng, e)
            return
        sem, val = ev
        if val <= 0 or self.waited[eng].get(sem, 0) >= val:
            return
        self.waited[eng][sem] = val
        self.q[eng].append(("w", sem, val))

    def op(self, eng, fn):
        self.q[eng].append(("o", fn, None, 0))

    def inc_last(self, eng, sem, n=1):
        tag, fn, inc, _ = self.q[eng][-1]
        assert tag == "o" and inc is None
        self.cnt[sem] = self.cnt.get(sem, 0) + n
        self.q[eng][-1] = ("o", fn, sem, n)
        return (sem, self.cnt[sem])

    def opi(self, eng, fn, sem, n=1):
        self.op(eng, fn)
        return self.inc_last(eng, sem, n)

    def val(self, sem):
        return (sem, self.cnt.get(sem, 0))



def build_nc():
    nc = bass.Bass("TRN2", target_bir_lowering=False, debug=False,
                   detect_race_conditions=DETECT_RACES)
    c = type("C", (), {})()
    c.nc = nc

    di = lambda n, sh, dt=F32: nc.dram_tensor(n, sh, dt, kind="ExternalInput").ap()
    c.x_fm = di("x_fm", [D, S])
    c.x_tm = di("x_tm", [S, D])
    c.hq = di("hq", [12, 128, 16, 512])
    c.hs = di("hs", [128, 128, 16, 128])
    c.hw = di("hw", [4, 8, 128, 8, 512])
    c.hu = di("hu", [128, 16, DFF])
    c.hg = di("hg", [128, 16, DFF])
    c.hd = di("hd", [128, 64, D])
    c.su_ug = di("su_ug", [128, 16])
    c.su_dp = di("su_dp", [128, 64])
    c.wsu_raw = di("wsu_raw", [128, 64])
    c.wsg_in = di("wsg", [128, 64])
    c.wsd_row = di("wsd_row", [1, D])
    c.iden = di("iden", [128, 128], BF16)
    c.idenf = di("idenf", [128, 128])
    c.constv = di("constv", [128, 4])
    c.onesq = di("onesq", [128, 128], BF16)
    c.ones_col = di("ones_col", [128, 1], BF16)
    c.ones_row = di("ones_row", [1, 128], BF16)
    c.ones_row_f = di("ones_row_f", [1, 128])
    c.utri = di("utri", [128, 128], BF16)
    c.h128 = di("h128", [128, 128], BF16)
    c.h16i8 = di("h16i8", [128, 128], BF16)
    c.h64i2 = di("h64i2", [128, 128], BF16)

    c.out = nc.dram_tensor("out", [S, D], F32, kind="ExternalOutput").ap()
    c.T1u = nc.dram_tensor("T1u", [D, DFF], BF16, kind="Internal").ap()
    c.T1g = nc.dram_tensor("T1g", [D, DFF], BF16, kind="Internal").ap()
    c.T1d = nc.dram_tensor("T1d", [DFF, D], BF16, kind="Internal").ap()
    c.WugS = nc.dram_tensor("WugS", [64, 128, 2, 16, 128], BF16, kind="Internal").ap()
    c.WqS2 = nc.dram_tensor("WqS2", [12, 2, 128, 8, 512], BF16, kind="Internal").ap()
    c.WsS2 = nc.dram_tensor("WsS2", [64, 128, 2, 16, 128], BF16, kind="Internal").ap()
    c.WoS = nc.dram_tensor("WoS", [4, 8, 128, 8, 512], BF16, kind="Internal").ap()
    c.WdS = nc.dram_tensor("WdS", [4, 8, 128, 8, 512], BF16, kind="Internal").ap()
    c.taps = {}
    shapes = {"xn": ([D, S], BF16), "qkv": ([S, DQKV], BF16),
              "y": ([D, S], BF16), "h": ([DFF, S], BF16),
              "mlp": ([S, D], BF16), "f": ([DFF, S], BF16),
              "cols": ([NT, 128, 8, TBN], F32)}
    shapes["t1u"] = ([D, DFF], BF16)
    shapes["car"] = ([NT, 2, D], BF16)
    shapes["wug"] = ([64, 128, 2, 16, 128], BF16)
    for t in DEBUG_TAPS:
        sh, dt = shapes[t]
        c.taps[t] = nc.dram_tensor("tap_" + t, sh, dt, kind="ExternalOutput").ap()

    SB = 16640
    A = lambda name, shape, dtype, offset: nc.alloc_sbuf_tensor_at(
        name, shape, dtype, offset=offset + SB)
    c.qkv_bf = A("qkv_bf", [128, TBN, DQKV], BF16, offset=0)
    c.wg_w = A("wg_w", [128, 2, TBN, 512], BF16, offset=49152)
    c.wg_wv = A("wg_wv", [128, 2, TBN, 512], BF16, offset=57344)
    c.h_fm = A("h_fm", [128, 64, T], BF16, offset=0)
    c.pre_in0 = A("pre_in0", [128, 8192], BF16, offset=0)
    c.pre_in1 = A("pre_in1", [128, 8192], BF16, offset=16384)
    c.pre_out0 = A("pre_out0", [128, 8192], BF16, offset=32768)
    c.pre_out1 = A("pre_out1", [128, 8192], BF16, offset=49152)
    c.pre_in0b = A("pre_in0b", [128, 8192], BF16, offset=65536)
    c.pre_out0b = A("pre_out0b", [128, 8192], BF16, offset=81920)
    c.pre_in1b = A("pre_in1b", [128, 8192], BF16, offset=98304)
    c.pre_out1b = A("pre_out1b", [128, 8192], BF16, offset=147456)
    c.x_tm_rot = A("x_tm_rot", [128, 2, D], F32, offset=0)
    c.oa_a = A("oa_a", [128, D], F32, offset=16384)
    c.oa_b = A("oa_b", [128, D], F32, offset=24576)
    c.wsd_tile = A("wsd_tile", [128, D], F32, offset=32768)
    c.x_fm_s = A("x_fm_s", [128, 16, T], F32, offset=65536)
    c.bm = A("bm", [128, T], F32, offset=98304)
    c.bs = A("bs", [128, T], F32, offset=100352)
    c.l_t1 = A("l_t1", [128, T], F32, offset=102400)
    c.l_t2 = A("l_t2", [128, T], F32, offset=104448)
    c.l_t3 = A("l_t3", [128, T], F32, offset=106496)
    c.l_t4 = A("l_t4", [128, T], F32, offset=108544)
    c.f1t = A("f1t", [128, T], F32, offset=110592)
    c.s1pt = A("s1pt", [128, T], F32, offset=112640)
    c.msnt = A("msnt", [128, T], F32, offset=114688)
    c.xc_rot = A("xc_rot", [128, 2, T], BF16, offset=116736)
    c.xs_rot = A("xs_rot", [128, 2, T], BF16, offset=118784)
    c.dn_rot = A("dn_rot", [128, 2, T], F32, offset=65536)
    c.rec_rot = A("rec_rot", [128, 2, T], F32, offset=69632)
    c.sig_rot = A("sig_rot", [128, 2, T], F32, offset=73728)
    c.t2_rot = A("t2_rot", [128, 2, T], F32, offset=77824)
    c.y_rot = A("y_rot", [128, 2, T], BF16, offset=81920)
    c.f_hi = A("f_hi", [128, 32, T], BF16, offset=65536)
    c.f_lo = A("f_lo", [128, 32, T], BF16, offset=98304)
    c.hsu = A("hsu", [128, 16, 128], BF16, offset=120832)
    c.pre_misc = A("pre_misc", [128, 128], BF16, offset=125184)
    c.warena = A("warena", [128, 2, 4096], BF16, offset=131072)
    c.xn_bf = A("xn_bf", [128, 16, T], BF16, offset=147456)
    c.qsq = A("qsq", [128, D], BF16, offset=163840)
    c.y_fm = A("y_fm", [128, 16, T], BF16, offset=163840)
    c.mlp_bf = A("mlp_bf", [128, TBN, D], BF16, offset=163840)
    c.car_kv = A("car_kv", [1, D], BF16, offset=180224)
    c.car_w = A("car_w", [1, D], BF16, offset=184320)
    c.silu_rot = A("silu_rot", [128, 2, 512], F32, offset=188416)
    off = [192512]

    def A7(name, shape, dtype):
        t = A(name, shape, dtype, offset=off[0])
        nbytes = int(np.prod(shape[1:])) * (2 if dtype == BF16 else 4)
        off[0] += (nbytes + 31) // 32 * 32
        return t
    c.s_iden = A7("s_iden", [128, 128], BF16)
    c.s_idenf = A7("s_idenf", [128, 128], F32)
    c.s_constv = A7("s_constv", [128, 4], F32)
    c.s_onesq = A7("s_onesq", [128, 128], BF16)
    c.s_onesc = A7("s_onesc", [128, 1], BF16)
    c.s_onesr = A7("s_onesr", [1, 128], BF16)
    c.s_onesrf = A7("s_onesrf", [1, 128], F32)
    c.s_utri = A7("s_utri", [128, 128], BF16)
    c.s_h128 = A7("s_h128", [128, 128], BF16)
    c.s_h16i8 = A7("s_h16i8", [128, 128], BF16)
    c.s_h64i2 = A7("s_h64i2", [128, 128], BF16)
    c.s_suug = A7("s_suug", [128, 16], F32)
    c.s_sudp = A7("s_sudp", [128, 64], F32)
    c.s_wsuraw = A7("s_wsuraw", [128, 64], F32)
    c.s_wsg = A7("s_wsg", [128, 64], F32)
    c.s_wsu = A7("s_wsu", [128, 64], F32)
    c.s_wsdr = A7("s_wsdr", [1, D], F32)
    c.col_sx = A7("col_sx", [128, TBN], F32)
    c.col_sx2 = A7("col_sx2", [128, TBN], F32)
    c.col_m = A7("col_m", [128, TBN], F32)
    c.col_t = A7("col_t", [128, TBN], F32)
    c.col_var = A7("col_var", [128, TBN], F32)
    c.col_u1 = A7("col_u1", [128, TBN], F32)
    c.col_f1 = A7("col_f1", [128, TBN], F32)
    c.col_s = A7("col_s", [128, TBN], F32)
    c.col_s1p = A7("col_s1p", [128, TBN], F32)
    c.col_msn = A7("col_msn", [128, TBN], F32)
    c.col_sq = A7("col_sq", [128, TBN], F32)
    c.col_sk = A7("col_sk", [128, TBN], F32)
    c.row_rot = A7("row_rot", [1, 512], F32)
    c.col_tq = A7("col_tq", [128, TBN], F32)
    c.col_tk = A7("col_tk", [128, TBN], F32)
    c.col_dbg = A7("col_dbg", [128, 8, TBN], F32)
    assert off[0] <= 212736, f"R7 overflow: {off[0]}"

    nc.const_aps.register(F32, EPS_LN, c.s_constv[:, 0:1])
    nc.const_aps.register(F32, EPS_RMS, c.s_constv[:, 1:2])
    nc.const_aps.register(F32, 1.0 / D, c.s_constv[:, 2:3])
    nc.const_aps.register(F32, 1.0, c.s_constv[:, 3:4])
    c.pb = [nc.alloc_psum_tensor(f"pb{i}", [128, 512], F32) for i in range(8)]
    c.pbb = [c.pb[i][:].bitcast(BF16) for i in range(8)]

    c.prog = Prog()
    return c


def emit_preamble(c):
    p = c.prog

    consts = [(c.s_iden, c.iden), (c.s_idenf, c.idenf), (c.s_constv, c.constv),
              (c.s_onesq, c.onesq), (c.s_onesc, c.ones_col),
              (c.s_onesr, c.ones_row), (c.s_onesrf, c.ones_row_f),
              (c.s_utri, c.utri), (c.s_h128, c.h128), (c.s_h16i8, c.h16i8),
              (c.s_h64i2, c.h64i2), (c.s_suug, c.su_ug), (c.s_sudp, c.su_dp),
              (c.s_wsuraw, c.wsu_raw), (c.s_wsg, c.wsg_in), (c.s_wsdr, c.wsd_row)]
    for dst, src in consts:
        p.op("sync", lambda e, d=dst, s=src: e.dma_start(out=d[:], in_=s))
        ev_c = p.inc_last("sync", "dK", 16)
    c.ev_consts = ev_c

    for ns in range(4):
        p.op("gpsimd", lambda e, ns=ns: e.dma_start(out=c.WoS[ns], in_=c.hw[ns]))
        p.inc_last("gpsimd", "dCast", 16)
    c.ev_cast_all = p.val("dCast")
    c.ev_cast_q = c.ev_cast_all
    c.ev_cast_s = c.ev_cast_all
    c.ev_cast_o = c.ev_cast_all
    c.ev_cast_h = c.ev_cast_all
    p.wait("vector", c.ev_consts)
    p.op("vector", lambda e: e.tensor_mul(c.s_wsu[:], c.s_wsuraw[:], c.s_sudp[:]))
    p.op("vector", lambda e: e.memset(c.car_kv[:], 0.0))
    ev_carz = p.opi("vector", lambda e: e.memset(c.car_w[:], 0.0), "sV")
    c.ev_carry_upd = [ev_carz] * NG

    ev_b7 = None
    for cc in range(16):
        ev_rs = p.opi("vector", lambda e, cc=cc: e.tensor_scalar(
            c.pre_misc[:], c.s_h128[:], c.s_suug[:, cc:cc + 1], None, op0=OP.mult), "sV")
        p.wait("tensor", ev_rs)
        p.wait("tensor", c.ev_consts)
        p.wait("tensor", ev_b7)
        ev_tp = p.opi("tensor", lambda e: e.transpose(
            c.pbb[7][:, 0:128], c.pre_misc[:], c.s_iden[:]), "sT")
        p.wait("scalar", ev_tp)
        ev_b7 = p.opi("scalar", lambda e, cc=cc: e.activation(
            c.hsu[:, cc, :], c.pbb[7][:, 0:128], AF.Copy), "sS")
        p.wait("vector", ev_tp)

    st_c = {"prein": {0: None, 1: None}, "preout": {0: None, 1: None},
            "bank": {0: None, 1: None, 4: None, 5: None}}
    st_w = {"prein": {0: None, 1: None}, "preout": {0: None, 1: None},
            "bank": {2: None, 3: None}}
    c_in = {0: c.pre_in0, 1: c.pre_in0b}
    c_out = {0: c.pre_out0, 1: c.pre_out0b}
    w_in = {0: c.pre_in1, 1: c.pre_in1b}
    w_out = {0: c.pre_out1, 1: c.pre_out1b}

    def cross_step(host_ap, lhsT, pblk, msl, t1_view):
        M = host_ap.shape[2]
        free = pblk * msl
        assert free == 8192
        nmm = free // 512
        last_store = None
        for sl in range(M // msl):
            h = sl % 2
            p.wait("gpsimd", st_c["prein"][h])
            srcv = host_ap[:, :, sl * msl:(sl + 1) * msl]
            p.op("gpsimd", lambda e, s=srcv, h=h: e.dma_start(
                out=c_in[h][:].rearrange("q (b m) -> q b m", b=pblk), in_=s))
            ev_load = p.inc_last("gpsimd", f"dPc{h}", 16)
            p.wait("tensor", ev_load)
            p.wait("tensor", st_c["preout"][h])
            for j in range(nmm):
                bk = j % 2 + 4 * h
                p.wait("tensor", st_c["bank"][bk])
                ev_mm = p.opi("tensor", lambda e, j=j, b=bk, h=h: e.matmul(
                    c.pb[b][:], lhsT, c_in[h][:, j * 512:(j + 1) * 512],
                    start=True, stop=True), "sT")
                p.wait("scalar", ev_mm)
                ev_ex = p.opi("scalar", lambda e, j=j, b=bk, h=h: e.activation(
                    c_out[h][:, j * 512:(j + 1) * 512], c.pb[b][:], AF.Copy), "sS")
                st_c["bank"][bk] = ev_ex
            st_c["prein"][h] = ev_mm
            p.wait("sync", ev_ex)
            npe = t1_view.shape[1]
            ncc = t1_view.shape[0]
            for ccx in range(ncc):
                dst = t1_view[ccx, :, :, sl * msl:(sl + 1) * msl]
                sv = c_out[h][ccx * npe:(ccx + 1) * npe, :].rearrange(
                    "q (b m) -> q b m", b=pblk)
                p.op("sync", lambda e, d=dst, s=sv: e.dma_start(out=d, in_=s))
                last_store = p.inc_last("sync", f"dPs{h}", 16)
            st_c["preout"][h] = last_store
        return [p.val("dPs0"), p.val("dPs1")]

    def within_step(t1_k, lhsT_for, scale, dest_fn, nkc, M, ev_t1_done):
        nmm = M // 512
        for cc in range(nkc):
            h = cc % 2
            p.wait("sync", ev_t1_done)
            p.wait("sync", st_w["prein"][h])
            srcv = t1_k[cc]
            p.op("sync", lambda e, s=srcv, M=M, h=h: e.dma_start(
                out=w_in[h][:, 0:M], in_=s))
            ev_load = p.inc_last("sync", f"dPl{h}", 16)
            p.wait("tensor", ev_load)
            p.wait("tensor", st_w["preout"][h])
            for j in range(nmm):
                bk = 2 + j % 2
                p.wait("tensor", st_w["bank"][bk])
                ev_mm = p.opi("tensor", lambda e, j=j, b=bk, cc=cc, h=h: e.matmul(
                    c.pb[b][:], lhsT_for(cc), w_in[h][:, j * 512:(j + 1) * 512],
                    start=True, stop=True), "sT")
                p.wait("scalar", ev_mm)
                ev_ex = p.opi("scalar", lambda e, j=j, b=bk, scale=scale, h=h: e.activation(
                    w_out[h][:, j * 512:(j + 1) * 512], c.pb[b][:], AF.Copy,
                    scale=scale), "sS")
                st_w["bank"][bk] = ev_ex
            st_w["prein"][h] = ev_mm
            p.wait("sync", ev_ex)
            dst, sv = dest_fn(cc, h)
            p.op("sync", lambda e, d=dst, s=sv: e.dma_start(out=d, in_=s))
            st_w["preout"][h] = p.inc_last("sync", f"dPo{h}", 16)

    p.wait("tensor", c.ev_consts)
    ev_hsu_done = p.val("sS")

    t1v = {}
    for nm, t1 in (("u", c.T1u), ("g", c.T1g)):
        t1v[nm] = t1.rearrange("(cc pb pe) m -> cc pe pb m", cc=16, pb=16, pe=8)
    t1v["d"] = c.T1d.rearrange("(cc pb pe) m -> cc pe pb m", cc=64, pb=64, pe=2)

    ev_t1 = {}
    ev_t1["u"] = cross_step(c.hu, c.s_h16i8[:], 16, 512, t1v["u"])

    def mk_dest_ug(hslot):
        def dest(cc, h):
            d = c.WugS[:, :, hslot, cc, :].rearrange("m q mm -> q m mm")
            s = w_out[h][:, 0:DFF].rearrange("q (m mm) -> q m mm", m=64)
            return d, s
        return dest

    def mk_tk(t1, nkc):
        return [t1[i * 128:(i + 1) * 128, :] for i in range(nkc)]

    within_step(mk_tk(c.T1u, 16), lambda cc: c.hsu[:, cc, :], RS2048,
                mk_dest_ug(0), 16, DFF, ev_t1["u"])
    ev_t1["g"] = cross_step(c.hg, c.s_h16i8[:], 16, 512, t1v["g"])
    within_step(mk_tk(c.T1g, 16), lambda cc: c.hsu[:, cc, :], RS2048,
                mk_dest_ug(1), 16, DFF, ev_t1["g"])
    ev_t1["d"] = cross_step(c.hd, c.s_h64i2[:], 64, 128, t1v["d"])

    def dest_dn(cc, h):
        d = c.WdS[:, cc // 8, :, cc % 8, :].rearrange("ns q mm -> q ns mm")
        s = w_out[h][:, 0:D].rearrange("q (ns mm) -> q ns mm", ns=4)
        return d, s
    within_step(mk_tk(c.T1d, 64), lambda cc: c.s_h128[:], RS8192,
                dest_dn, 64, D, ev_t1["d"])

    for bk, ev in list(st_c["bank"].items()) + list(st_w["bank"].items()):
        c.ev_bank0_pre = None
    if "t1u" in c.taps:
        p.wait("gpsimd", ev_t1["u"])
        p.op("gpsimd", lambda e: e.dma_start(out=c.taps["t1u"], in_=c.T1u))
        p.inc_last("gpsimd", "dTapG", 16)
    if "wug" in c.taps:
        p.wait("gpsimd", p.val("dS"))
        p.op("gpsimd", lambda e: e.dma_start(out=c.taps["wug"], in_=c.WugS))
        p.inc_last("gpsimd", "dTapG", 16)
    c.ev_pre_t = p.val("sT")
    c.ev_pre_s = p.val("sS")
    c.ev_pre_sync = [p.val("dPs0"), p.val("dPs1"), p.val("dPo0"), p.val("dPo1"),
                     p.val("dPl0"), p.val("dPl1")]


def emit_g_pair(c, m, bu, bg, it):
    p = c.prog
    buf = m % 2
    if m == 0:
        p.wait("sync", c.ev_pre_sync)
    p.wait("sync", c.ev_war_free[buf])
    src = c.WugS[m].rearrange("q h kc mm -> q (h kc mm)")
    p.op("sync", lambda e, s=src, b=buf: e.dma_start(out=c.warena[:, b], in_=s))
    ev_w = p.inc_last("sync", f"dW{buf}", 16)
    p.wait("tensor", ev_w)
    wv4 = c.warena[:, buf].rearrange("q (h kc m) -> q h kc m", h=2, kc=16)
    bank = c.ev_bank
    p.wait("tensor", bank[bu])
    p.wait("tensor", bank[bg])
    for kc in range(16):
        p.op("tensor", lambda e, kc=kc, bu=bu, wv4=wv4: e.matmul(
            c.pb[bu][:], wv4[:, 0, kc, :], c.xn_bf[:, kc, :],
            start=(kc == 0), stop=(kc == 15)))
    for kc in range(16):
        p.op("tensor", lambda e, kc=kc, bg=bg, wv4=wv4: e.matmul(
            c.pb[bg][:], wv4[:, 1, kc, :], c.xn_bf[:, kc, :],
            start=(kc == 0), stop=(kc == 15)))
    ev_mm = p.inc_last("tensor", "sT")
    c.ev_war_free[buf] = ev_mm
    rot = m % 2
    p.wait("scalar", ev_mm)
    p.wait("scalar", c.ev_silu_free[rot])
    ev_si = p.opi("scalar", lambda e, bg=bg, rot=rot, m=m: e.activation(
        c.silu_rot[:, rot, :], c.pb[bg][:], AF.Silu,
        scale=c.s_wsg[:, m:m + 1]), "sS")
    p.wait("vector", ev_si)
    fdst = c.f_lo[:, m, :] if m < 32 else c.f_hi[:, m - 32, :]
    ev_f = p.opi("vector", lambda e, fdst=fdst, bu=bu, rot=rot, m=m: e.scalar_tensor_tensor(
        fdst, c.pb[bu][:], c.s_wsu[:, m:m + 1],
        c.silu_rot[:, rot, :], op0=OP.mult, op1=OP.mult), "sV")
    c.ev_silu_free[rot] = ev_f
    bank[bu] = ev_f
    bank[bg] = ev_si


def emit_tile(c, it):
    p = c.prog
    t0 = it * T
    bank = c.ev_bank

    p.wait("gpsimd", c.ev_r2_free)
    p.op("gpsimd", lambda e: e.dma_start(
        out=c.x_fm_s[:],
        in_=c.x_fm[:, t0:t0 + T].rearrange("(kc q) t -> q kc t", q=128)))
    ev_x = p.inc_last("gpsimd", "dX", 16)

    p.wait("scalar", ev_x)
    p.wait("scalar", c.ev_f_free)
    ev_conv = {0: None, 1: None}
    for cc in range(16):
        r = cc % 2
        p.wait("scalar", c.ev_lrot_free[r])
        p.op("scalar", lambda e, cc=cc, r=r: e.activation(
            c.xc_rot[:, r, :], c.x_fm_s[:, cc, :], AF.Copy))
        ev_conv[r] = p.opi("scalar", lambda e, cc=cc, r=r: e.activation(
            c.xs_rot[:, r, :], c.x_fm_s[:, cc, :], AF.Square), "sS")
        p.wait("tensor", ev_conv[r])
        if cc == 0:
            p.wait("tensor", bank[0])
            p.wait("tensor", bank[1])
        p.op("tensor", lambda e, r=r, cc=cc: e.matmul(
            c.pb[0][:], c.s_onesq[:], c.xc_rot[:, r, :],
            start=(cc == 0), stop=(cc == 15), skip_group_check=True))
        p.op("tensor", lambda e, r=r, cc=cc: e.matmul(
            c.pb[1][:], c.s_onesq[:], c.xs_rot[:, r, :],
            start=(cc == 0), stop=(cc == 15), skip_group_check=True))
        ev_cc = p.inc_last("tensor", "sT")
        c.ev_lrot_free[r] = ev_cc
    ev_stats = ev_cc

    p.wait("vector", ev_stats)
    p.wait("vector", c.ev_ltiles_free)
    p.op("vector", lambda e: e.tensor_scalar(c.bm[:], c.pb[0][:], 1.0 / D, None, op0=OP.mult))
    p.op("vector", lambda e: e.tensor_mul(c.l_t1[:], c.bm[:], c.bm[:]))
    ev_var = p.opi("vector", lambda e: e.scalar_tensor_tensor(
        c.l_t2[:], c.pb[1][:], 1.0 / D, c.l_t1[:], op0=OP.mult, op1=OP.subtract), "sV")
    bank[0] = ev_var
    bank[1] = ev_var
    p.wait("scalar", ev_var)
    p.wait("scalar", c.ev_consts)
    ev_sd = p.opi("scalar", lambda e: e.activation(
        c.l_t1[:], c.l_t2[:], AF.Sqrt, bias=EPS_LN), "sS")
    p.wait("vector", ev_sd)
    ev_bs = p.opi("vector", lambda e: e.reciprocal(c.bs[:], c.l_t1[:]), "sV")
    p.op("vector", lambda e: e.tensor_mul(c.l_t3[:], c.bs[:], c.bs[:]))
    ev_r2 = p.opi("vector", lambda e: e.tensor_mul(c.l_t4[:], c.l_t3[:], c.l_t2[:]), "sV")
    p.wait("scalar", ev_r2)
    ev_sf = p.opi("scalar", lambda e: e.activation(
        c.l_t3[:], c.l_t4[:], AF.Sqrt, bias=EPS_RMS), "sS")
    p.wait("vector", ev_sf)
    p.op("vector", lambda e: e.reciprocal(c.f1t[:], c.l_t3[:]))
    p.op("vector", lambda e: e.tensor_scalar(c.s1pt[:], c.bs[:], 1.0, None, op0=OP.add))
    ev_tiles = p.opi("vector", lambda e: e.scalar_tensor_tensor(
        c.msnt[:], c.bm[:], -1.0, c.bs[:], op0=OP.mult, op1=OP.mult), "sV")
    p.wait("tensor", ev_tiles)
    ev_f1 = None
    for qi, (tile, colt) in enumerate(((c.f1t, c.col_f1), (c.s1pt, c.col_s1p),
                                       (c.msnt, c.col_msn))):
        bt = 2 + qi % 2
        p.wait("tensor", bank[bt])
        for tb in range(TBN):
            p.op("tensor", lambda e, tile=tile, tb=tb, bt=bt: e.transpose(
                c.pb[bt][:, tb * 128:(tb + 1) * 128],
                tile[:, tb * 128:(tb + 1) * 128], c.s_idenf[:]))
        ev_tp = p.inc_last("tensor", "sT")
        p.wait("scalar", ev_tp)
        for tb in range(TBN):
            p.op("scalar", lambda e, colt=colt, tb=tb, bt=bt: e.activation(
                colt[:, tb:tb + 1], c.pb[bt][:, tb * 128:tb * 128 + 1], AF.Copy))
        ev_f1 = p.inc_last("scalar", "sS")
        bank[bt] = ev_f1
    ev_s1p = ev_f1
    c.ev_ltiles_free = None
    if "cols" in c.taps:
        p.wait("vector", ev_f1)
        p.op("vector", lambda e: e.tensor_copy(c.col_dbg[:, 2, :], c.col_f1[:]))
        p.op("vector", lambda e: e.tensor_copy(c.col_dbg[:, 3, :], c.col_s1p[:]))
        ev_dbg = p.opi("vector", lambda e: e.tensor_copy(
            c.col_dbg[:, 4, :], c.col_msn[:]), "sV")
        p.wait("sync", ev_dbg)
        p.op("sync", lambda e, it=it: e.dma_start(
            out=c.taps["cols"][it], in_=c.col_dbg[:]))
        p.inc_last("sync", "dTap", 16)

    p.wait("vector", ev_bs)
    p.wait("vector", c.ev_xn_free)
    bmb = c.bm[:].unsqueeze(1).broadcast_to([128, 16, T])
    bsb = c.bs[:].unsqueeze(1).broadcast_to([128, 16, T])
    p.op("vector", lambda e, bmb=bmb: e.tensor_tensor(
        c.x_fm_s[:], c.x_fm_s[:], bmb, op=OP.subtract))
    ev_xn = p.opi("vector", lambda e, bsb=bsb: e.tensor_tensor(
        c.xn_bf[:], c.x_fm_s[:], bsb, op=OP.mult), "sV")
    c.ev_ltiles_free = ev_xn
    if "xn" in c.taps:
        p.wait("sync", ev_xn)
        p.op("sync", lambda e: e.dma_start(
            out=c.taps["xn"][:, t0:t0 + T].rearrange("(kc q) t -> q kc t", q=128),
            in_=c.xn_bf[:]))
        p.inc_last("sync", "dTap", 16)

    ev_q_ns = {}
    for ns in range(12):
        bank0 = (ns % 2) * 4
        for half in range(2):
            buf = half
            if it == 0:
                p.wait("gpsimd", c.ev_war_free[buf])
                p.wait("gpsimd", c.ev_war_stored[buf])
                src = c.hq[ns, :, half * 8:(half + 1) * 8, :]
                p.op("gpsimd", lambda e, s=src, b=buf: e.dma_start(
                    out=c.warena[:, b].rearrange("q (kc m) -> q kc m", kc=8), in_=s))
                ev_w = p.inc_last("gpsimd", f"dWg{buf}", 16)
                p.wait("sync", ev_w)
                p.op("sync", lambda e, ns=ns, half=half, b=buf: e.dma_start(
                    out=c.WqS2[ns, half], in_=c.warena[:, b].rearrange(
                        "q (kc m) -> q kc m", kc=8)))
                c.ev_war_stored[buf] = p.inc_last("sync", f"dB{buf}", 16)
            else:
                if it == 1 and ns == 0:
                    p.wait("sync", p.val("dB0"))
                    p.wait("sync", p.val("dB1"))
                p.wait("sync", c.ev_war_free[buf])
                src = c.WqS2[ns, half].rearrange("q kc m -> q (kc m)")
                p.op("sync", lambda e, s=src, b=buf: e.dma_start(
                    out=c.warena[:, b], in_=s))
                ev_w = p.inc_last("sync", f"dW{buf}", 16)
            p.wait("tensor", ev_w)
            if ns == 0 and half == 0:
                p.wait("tensor", ev_xn)
            for tb in range(TBN):
                bk = bank0 + tb
                if half == 0:
                    p.wait("tensor", bank[bk])
                for kc8 in range(8):
                    kc = half * 8 + kc8
                    p.op("tensor", lambda e, tb=tb, kc=kc, kc8=kc8, b=buf, bk=bk, half=half: e.matmul(
                        c.pb[bk][:],
                        c.xn_bf[:, kc, tb * 128:(tb + 1) * 128],
                        c.warena[:, b].rearrange("q (kc m) -> q kc m", kc=8)[:, kc8, :],
                        start=(half == 0 and kc8 == 0), stop=(half == 1 and kc8 == 7)))
            ev_half = p.inc_last("tensor", "sT")
            c.ev_war_free[buf] = ev_half
        ev_ns = ev_half
        p.wait("scalar", ev_ns)
        p.wait("scalar", ev_f1)
        if ns == 0:
            p.wait("scalar", c.ev_qkvbuf_free)
        for tb in range(TBN):
            bk = bank0 + tb
            p.op("scalar", lambda e, ns=ns, tb=tb, bk=bk: e.activation(
                c.qkv_bf[:, tb, ns * 512:(ns + 1) * 512], c.pb[bk][:], AF.Copy,
                scale=c.col_f1[:, tb:tb + 1]))
            bank[bk] = p.inc_last("scalar", "sS")
        ev_q_ns[ns] = bank[bank0 + 3]
    if "qkv" in c.taps:
        p.wait("sync", p.val("sS"))
        p.op("sync", lambda e: e.dma_start(
            out=c.taps["qkv"][t0:t0 + T, :].rearrange("(tb q) m -> q tb m", q=128),
            in_=c.qkv_bf[:]))
        p.inc_last("sync", "dTap", 16)

    p.wait("scalar", ev_q_ns[7])
    p.wait("scalar", c.ev_r5_free)
    for tb in range(TBN):
        p.op("scalar", lambda e, tb=tb: e.activation(
            c.qsq[:], c.qkv_bf[:, tb, 0:D], AF.Square,
            accum_out=c.col_sx[:, tb:tb + 1]))
        p.op("scalar", lambda e, tb=tb: e.activation(
            c.qsq[:], c.qkv_bf[:, tb, D:2 * D], AF.Square,
            accum_out=c.col_sx2[:, tb:tb + 1]))
    ev_sq = p.inc_last("scalar", "sS")
    p.op("scalar", lambda e: e.activation(
        c.col_tq[:], c.col_sx[:], AF.Sqrt, scale=1.0 / D, bias=EPS_RMS))
    ev_sk1 = p.opi("scalar", lambda e: e.activation(
        c.col_tk[:], c.col_sx2[:], AF.Sqrt, scale=1.0 / D, bias=EPS_RMS), "sS")
    p.wait("vector", ev_sk1)
    p.op("vector", lambda e: e.reciprocal(c.col_sq[:], c.col_tq[:]))
    ev_qkcols_v = p.opi("vector", lambda e: e.reciprocal(c.col_sk[:], c.col_tk[:]), "sV")
    ev_qkcols = ev_qkcols_v

    ev_T_pe_last = None
    for g in range(NG):
        gr = g % 2
        p.wait("scalar", ev_qkcols)
        p.wait("scalar", ev_q_ns[8 + g])
        p.wait("scalar", c.ev_wg_free[gr])
        for tb in range(TBN):
            p.op("scalar", lambda e, g=g, tb=tb, gr=gr: e.activation(
                c.wg_w[:, gr, tb, :], c.qkv_bf[:, tb, D + g * 512:D + (g + 1) * 512],
                AF.Exp, scale=c.col_sk[:, tb:tb + 1]))
        ev_w = p.inc_last("scalar", "sS")
        p.wait("vector", ev_w)
        vsl = c.qkv_bf[:].transpose([0, 1, 2])[:, :, 2 * D + g * 512:2 * D + (g + 1) * 512]
        p.op("vector", lambda e, g=g, gr=gr, vsl=vsl: e.tensor_mul(
            c.wg_wv[:, gr], c.wg_w[:, gr], vsl))
        ev_wv = p.inc_last("vector", "sV")

        p.wait("tensor", ev_wv)
        p.wait("tensor", c.ev_carry_upd[g])
        bkv, bw = gr * 2, gr * 2 + 1
        for tb in range(TBN):
            rot = tb % 2
            p.wait("tensor", bank[bkv])
            p.wait("tensor", bank[bw])
            for bk, dat, car in ((bkv, c.wg_wv, c.car_kv), (bw, c.wg_w, c.car_w)):
                p.op("tensor", lambda e, bk=bk, d=dat, tb=tb, gr=gr: e.matmul(
                    c.pb[bk][:], c.s_utri[:], d[:, gr, tb, :], start=True, stop=False))
                for tbp in range(tb):
                    p.op("tensor", lambda e, bk=bk, d=dat, tbp=tbp, gr=gr: e.matmul(
                        c.pb[bk][:], c.s_onesq[:], d[:, gr, tbp, :],
                        start=False, stop=False))
                if NOCARRY:
                    p.op("tensor", lambda e, bk=bk, d=dat, tb=tb, gr=gr: e.matmul(
                        c.pb[bk][:], c.s_utri[:], d[:, gr, tb, :], start=False, stop=True))
                else:
                    p.op("tensor", lambda e, bk=bk, car=car, g=g: e.matmul(
                        c.pb[bk][:], c.s_onesr[:], car[0:1, g * 512:(g + 1) * 512],
                        start=False, stop=True))
            ev_cs = p.inc_last("tensor", "sT")
            if tb == 0:
                p.wait("tensor", bank[4])
                p.wait("tensor", bank[5])
            p.op("tensor", lambda e, tb=tb, gr=gr: e.matmul(
                c.pb[4][:], c.s_onesq[:], c.wg_wv[:, gr, tb, :],
                start=(tb == 0), stop=(tb == 3)))
            p.op("tensor", lambda e, tb=tb, gr=gr: e.matmul(
                c.pb[5][:], c.s_onesq[:], c.wg_w[:, gr, tb, :],
                start=(tb == 0), stop=(tb == 3)))
            if tb == 3:
                ev_tot = p.inc_last("tensor", "sT")

            p.wait("scalar", ev_cs)
            p.wait("scalar", c.ev_trot_free[rot])
            p.op("scalar", lambda e, bw=bw, rot=rot: e.activation(
                c.dn_rot[:, rot, :], c.pb[bw][:], AF.Copy, bias=EPS_RATIO))
            p.op("scalar", lambda e, g=g, tb=tb, rot=rot: e.activation(
                c.sig_rot[:, rot, :], c.qkv_bf[:, tb, g * 512:(g + 1) * 512],
                AF.Sigmoid, scale=c.col_sq[:, tb:tb + 1]))
            ev_dn = p.inc_last("scalar", "sS")
            p.wait("vector", ev_dn)
            p.wait("vector", c.ev_yrot_free[rot])
            p.op("vector", lambda e, rot=rot: e.reciprocal(
                c.rec_rot[:, rot, :], c.dn_rot[:, rot, :]))
            p.op("vector", lambda e, bkv=bkv, rot=rot: e.tensor_mul(
                c.t2_rot[:, rot, :], c.pb[bkv][:], c.rec_rot[:, rot, :]))
            p.op("vector", lambda e, rot=rot: e.tensor_mul(
                c.y_rot[:, rot, :], c.t2_rot[:, rot, :], c.sig_rot[:, rot, :]))
            ev_y = p.inc_last("vector", "sV")
            c.ev_trot_free[rot] = ev_y
            bank[bkv] = ev_y
            bank[bw] = ev_dn

            p.wait("tensor", ev_y)
            if tb == 0:
                p.wait("tensor", bank[6])
                p.wait("tensor", bank[7])
            for kcs in range(4):
                bt = 6 if kcs < 2 else 7
                colbase = (kcs % 2) * 512
                p.op("tensor", lambda e, rot=rot, kcs=kcs, tb=tb, bt=bt, colbase=colbase: e.transpose(
                    c.pbb[bt][:, colbase + tb * 128:colbase + (tb + 1) * 128],
                    c.y_rot[:, rot, kcs * 128:(kcs + 1) * 128], c.s_iden[:]))
            ev_tp = p.inc_last("tensor", "sT")
            ev_T_pe_last = ev_tp
            c.ev_yrot_free[rot] = ev_tp


        p.wait("scalar", ev_tp)
        p.wait("scalar", c.ev_r5_free)
        for kcs in range(4):
            bt = 6 if kcs < 2 else 7
            colbase = (kcs % 2) * 512
            p.op("scalar", lambda e, g=g, kcs=kcs, bt=bt, colbase=colbase: e.activation(
                c.y_fm[:, g * 4 + kcs, :], c.pbb[bt][:, colbase:colbase + 512], AF.Copy))
        ev_yevac = p.inc_last("scalar", "sS")
        bank[6] = ev_yevac
        bank[7] = ev_yevac
        p.wait("scalar", ev_tot)
        p.wait("scalar", c.ev_rowrot_free)
        ev_rk = p.opi("scalar", lambda e: e.activation(
            c.row_rot[:], c.pb[4][0:1, :], AF.Copy), "sS")
        p.wait("vector", ev_rk)
        ev_ck = p.opi("vector", lambda e, g=g: e.tensor_add(
            c.car_kv[0:1, g * 512:(g + 1) * 512],
            c.car_kv[0:1, g * 512:(g + 1) * 512], c.row_rot[:]), "sV")
        p.wait("scalar", ev_ck)
        ev_rw = p.opi("scalar", lambda e: e.activation(
            c.row_rot[:], c.pb[5][0:1, :], AF.Copy), "sS")
        p.wait("vector", ev_rw)
        ev_cw = p.opi("vector", lambda e, g=g: e.tensor_add(
            c.car_w[0:1, g * 512:(g + 1) * 512],
            c.car_w[0:1, g * 512:(g + 1) * 512], c.row_rot[:]), "sV")
        c.ev_carry_upd[g] = ev_cw
        c.ev_rowrot_free = ev_cw
        bank[4] = ev_rk
        bank[5] = ev_rw
        c.ev_wg_free[gr] = ev_tot

    c.ev_qkvbuf_free = [p.val("sS"), p.val("sV")]
    if "y" in c.taps:
        p.wait("sync", p.val("sS"))
        p.op("sync", lambda e: e.dma_start(
            out=c.taps["y"][:, t0:t0 + T].rearrange("(kc q) t -> q kc t", q=128),
            in_=c.y_fm[:]))
        p.inc_last("sync", "dTap", 16)

    ev_yfm_ready = p.val("sS")
    for m in range(64):
        buf = m % 2
        if it == 0:
            p.wait("gpsimd", c.ev_war_free[buf])
            p.wait("gpsimd", c.ev_war_stored[buf])
            for hs_, mall in ((0, m), (1, m + 64)):
                src = c.hs[mall]
                p.op("gpsimd", lambda e, s=src, b=buf, hs_=hs_: e.dma_start(
                    out=c.warena[:, b].rearrange("q (h kc m) -> q h kc m", h=2, kc=16)[:, hs_],
                    in_=s))
                ev_w = p.inc_last("gpsimd", f"dWg{buf}", 16)
            p.wait("sync", ev_w)
            p.op("sync", lambda e, m=m, b=buf: e.dma_start(
                out=c.WsS2[m], in_=c.warena[:, b].rearrange(
                    "q (h kc m) -> q h kc m", h=2, kc=16)))
            c.ev_war_stored[buf] = p.inc_last("sync", f"dB{buf}", 16)
        else:
            p.wait("sync", c.ev_war_free[buf])
            src = c.WsS2[m].rearrange("q h kc mm -> q (h kc mm)")
            p.op("sync", lambda e, s=src, b=buf: e.dma_start(
                out=c.warena[:, b], in_=s))
            ev_w = p.inc_last("sync", f"dW{buf}", 16)
        p.wait("tensor", ev_w)
        if m == 0:
            p.wait("tensor", ev_yfm_ready)
            p.wait("tensor", c.ev_h_free)
        bu, bg = (0, 1) if m % 2 == 0 else (2, 3)
        p.wait("tensor", bank[bu])
        p.wait("tensor", bank[bg])
        wv4 = c.warena[:, buf].rearrange("q (h kc m) -> q h kc m", h=2, kc=16)
        for kc in range(16):
            p.op("tensor", lambda e, kc=kc, bu=bu, wv4=wv4: e.matmul(
                c.pb[bu][:], wv4[:, 0, kc, :], c.y_fm[:, kc, :],
                start=(kc == 0), stop=(kc == 15)))
        for kc in range(16):
            p.op("tensor", lambda e, kc=kc, bg=bg, wv4=wv4: e.matmul(
                c.pb[bg][:], wv4[:, 1, kc, :], c.y_fm[:, kc, :],
                start=(kc == 0), stop=(kc == 15)))
        ev_mm = p.inc_last("tensor", "sT")
        c.ev_war_free[buf] = ev_mm
        rot = m % 2
        p.wait("scalar", ev_mm)
        p.wait("scalar", c.ev_silu_free[rot])
        ev_si = p.opi("scalar", lambda e, bg=bg, rot=rot: e.activation(
            c.silu_rot[:, rot, :], c.pb[bg][:], AF.Silu), "sS")
        p.wait("vector", ev_si)
        ev_h = p.opi("vector", lambda e, m=m, bu=bu, rot=rot: e.scalar_tensor_tensor(
            c.h_fm[:, m, :], c.pb[bu][:], 1.0, c.silu_rot[:, rot, :],
            op0=OP.mult, op1=OP.mult), "sV")
        c.ev_silu_free[rot] = ev_h
        bank[bu] = ev_h
        bank[bg] = ev_si
    ev_h_all = p.val("sV")
    if "h" in c.taps:
        p.wait("sync", ev_h_all)
        p.op("sync", lambda e: e.dma_start(
            out=c.taps["h"][:, t0:t0 + T].rearrange("(kc q) t -> q kc t", q=128),
            in_=c.h_fm[:]))
        p.inc_last("sync", "dTap", 16)

    for tbp in range(2):
        tbs = (tbp * 2, tbp * 2 + 1)
        for ns in range(4):
            for kcg in range(8):
                buf = (ns * 8 + kcg) % 2
                if tbp == 0 and ns == 0 and kcg == 0:
                    p.wait("sync", c.ev_cast_o)
                p.wait("sync", c.ev_war_free[buf])
                src = c.WoS[ns, kcg]
                p.op("sync", lambda e, s=src, b=buf: e.dma_start(
                    out=c.warena[:, b].rearrange("q (kc m) -> q kc m", kc=8), in_=s))
                ev_w = p.inc_last("sync", f"dW{buf}", 16)
                p.wait("tensor", ev_w)
                if tbp == 0 and ns == 0 and kcg == 0:
                    p.wait("tensor", ev_h_all)
                for ti, tb in enumerate(tbs):
                    bk = ti * 4 + ns
                    if kcg == 0:
                        p.wait("tensor", bank[bk])
                    for kc in range(8):
                        p.op("tensor", lambda e, kcg=kcg, kc=kc, tb=tb, bk=bk, b=buf: e.matmul(
                            c.pb[bk][:],
                            c.h_fm[:, kcg * 8 + kc, tb * 128:(tb + 1) * 128],
                            c.warena[:, b].rearrange("q (kc m) -> q kc m", kc=8)[:, kc, :],
                            start=(kcg == 0 and kc == 0), stop=(kcg == 7 and kc == 7)))
                ev_mm = p.inc_last("tensor", "sT")
                c.ev_war_free[buf] = ev_mm
        p.wait("scalar", ev_mm)
        if tbp == 0:
            p.wait("scalar", c.ev_r5_free_mlp)
        for ti, tb in enumerate(tbs):
            for ns in range(4):
                bk = ti * 4 + ns
                p.op("scalar", lambda e, tb=tb, ns=ns, bk=bk: e.activation(
                    c.mlp_bf[:, tb, ns * 512:(ns + 1) * 512], c.pb[bk][:], AF.Copy))
                bank[bk] = p.inc_last("scalar", "sS")
    ev_mlp = p.val("sS")
    c.ev_h_free = p.val("sT")
    if "mlp" in c.taps:
        p.wait("sync", ev_mlp)
        p.op("sync", lambda e: e.dma_start(
            out=c.taps["mlp"][t0:t0 + T, :].rearrange("(tb q) m -> q tb m", q=128),
            in_=c.mlp_bf[:]))
        p.inc_last("sync", "dTap", 16)

    for m in range(64):
        bu, bg = (0, 1) if m % 2 == 0 else (2, 3)
        emit_g_pair(c, m, bu, bg, it)
    ev_f_all = p.val("sV")
    c.ev_xn_free = p.val("sT")


    p.wait("tensor", c.ev_h_free)
    for j in range(4):
        p.wait("tensor", bank[7])
        ev_b = p.opi("tensor", lambda e, j=j: e.matmul(
            c.pb[7][:], c.s_onesrf[:], c.s_wsdr[0:1, j * 512:(j + 1) * 512],
            start=True, stop=True), "sT")
        p.wait("scalar", ev_b)
        p.op("scalar", lambda e, j=j: e.activation(
            c.wsd_tile[:, j * 512:(j + 1) * 512], c.pb[7][:], AF.Copy))
        bank[7] = p.inc_last("scalar", "sS")
    ev_wsdt = p.val("sS")

    for tbp in range(2):
        tbs = (tbp * 2, tbp * 2 + 1)
        for ti, tb in enumerate(tbs):
            rot = tb % 2
            p.wait("gpsimd", c.ev_xtm_free[rot])
            p.wait("gpsimd", c.ev_h_free)
            p.op("gpsimd", lambda e, tb=tb, rot=rot: e.dma_start(
                out=c.x_tm_rot[:, rot, :],
                in_=c.x_tm[t0 + tb * 128:t0 + (tb + 1) * 128, :]))
            c.ev_xtm_load[rot] = p.inc_last("gpsimd", f"dX{rot}", 16)
        for ns in range(4):
            for kcg in range(8):
                buf = (ns * 8 + kcg) % 2
                p.wait("sync", c.ev_pre_sync)
                p.wait("sync", c.ev_war_free[buf])
                src = c.WdS[ns, kcg].rearrange("q kc mm -> q (kc mm)")
                p.op("sync", lambda e, s=src, b=buf: e.dma_start(
                    out=c.warena[:, b], in_=s))
                ev_w = p.inc_last("sync", f"dW{buf}", 16)
                p.wait("tensor", ev_w)
                for ti, tb in enumerate(tbs):
                    bk = ti * 4 + ns
                    if kcg == 0:
                        p.wait("tensor", bank[bk])
                    for kc in range(8):
                        mk = kcg * 8 + kc
                        fsrc = (c.f_lo if mk < 32 else c.f_hi)
                        mkk = mk if mk < 32 else mk - 32
                        p.op("tensor", lambda e, mkk=mkk, tb=tb, bk=bk, b=buf, fsrc=fsrc, kcg=kcg, kc=kc: e.matmul(
                            c.pb[bk][:],
                            fsrc[:, mkk, tb * 128:(tb + 1) * 128],
                            c.warena[:, b].rearrange("q (kc m) -> q kc m", kc=8)[:, kc, :],
                            start=(kcg == 0 and kc == 0), stop=(kcg == 7 and kc == 7)))
                ev_mm = p.inc_last("tensor", "sT")
                c.ev_war_free[buf] = ev_mm
        for ti, tb in enumerate(tbs):
            rot = tb % 2
            p.wait("vector", ev_mm)
            p.wait("vector", ev_wsdt)
            p.wait("vector", c.ev_xtm_load[rot])
            p.wait("vector", ev_mlp)
            p.wait("vector", c.ev_oa_free)
            p.op("vector", lambda e, rot=rot, tb=tb: e.tensor_scalar(
                c.oa_a[:], c.x_tm_rot[:, rot, :], c.col_s1p[:, tb:tb + 1],
                c.col_msn[:, tb:tb + 1], op0=OP.mult, op1=OP.add))
            p.op("vector", lambda e, tb=tb: e.tensor_add(
                c.oa_b[:], c.oa_a[:], c.mlp_bf[:, tb, :]))
            for ns in range(4):
                bk = ti * 4 + ns
                p.op("vector", lambda e, bk=bk, ns=ns: e.tensor_mul(
                    c.oa_a[:, ns * 512:(ns + 1) * 512], c.pb[bk][:],
                    c.wsd_tile[:, ns * 512:(ns + 1) * 512]))
            ev_out = p.opi("vector", lambda e: e.tensor_add(
                c.oa_a[:], c.oa_a[:], c.oa_b[:]), "sV")
            for ns in range(4):
                bank[ti * 4 + ns] = ev_out
            c.ev_xtm_free[rot] = ev_out
            p.wait("gpsimd", ev_out)
            p.op("gpsimd", lambda e, tb=tb: e.dma_start(
                out=c.out[t0 + tb * 128:t0 + (tb + 1) * 128, :], in_=c.oa_a[:]))
            c.ev_oa_free = p.inc_last("gpsimd", "dO", 16)

    if "car" in c.taps:
        p.wait("sync", c.ev_carry_upd[3])
        p.op("sync", lambda e, it=it: e.dma_start(out=c.taps["car"][it, 0], in_=c.car_kv[:]))
        p.inc_last("sync", "dTap", 16)
        p.op("sync", lambda e, it=it: e.dma_start(out=c.taps["car"][it, 1], in_=c.car_w[:]))
        p.inc_last("sync", "dTap", 16)
    c.ev_f_free = p.val("sT")
    c.ev_r2_free = c.ev_f_free
    c.ev_r5_free = p.val("sV")
    c.ev_r5_free_mlp = p.val("sT")


def emit_program(c):
    p = c.prog
    c.ev_bank = {i: None for i in range(8)}
    c.ev_r2_free = None
    c.ev_lrot_free = {0: None, 1: None}
    c.ev_xn_free = None
    c.ev_qkvbuf_free = None
    c.ev_war_free = {0: None, 1: None}
    c.ev_wg_free = {0: None, 1: None}
    c.ev_trot_free = {0: None, 1: None}
    c.ev_yrot_free = {0: None, 1: None}
    c.ev_rowrot_free = None
    c.ev_silu_free = {0: None, 1: None}
    c.ev_h_free = None
    c.ev_f_free = None
    c.ev_oa_free = None
    c.ev_xtm_free = {0: None, 1: None}
    c.ev_war_stored = {0: None, 1: None}
    c.ev_xtm_load = {0: None, 1: None}
    c.ev_r5_free = None
    c.ev_r5_free_mlp = None
    c.ev_ltiles_free = None

    emit_preamble(c)
    c.ev_r2_free = c.ev_pre_sync
    c.ev_qkvbuf_free = [c.ev_pre_s, c.ev_pre_sync, c.ev_pre_t]
    c.ev_h_free = c.ev_pre_t
    c.ev_f_free = c.ev_pre_t
    for i in range(8):
        c.ev_bank[i] = c.ev_pre_s

    for it in range(NT):
        emit_tile(c, it)

    for s in ("dK", "dPs0", "dPs1", "dPl0", "dPl1", "dPo0", "dPo1",
              "dW0", "dW1", "dB0", "dB1", "dTap"):
        p.wait("sync", p.val(s))
    for s in ("dCast", "dPc0", "dPc1", "dX", "dX0", "dX1", "dO", "dTapG",
              "dWg0", "dWg1"):
        p.wait("gpsimd", p.val(s))


def replay(c):
    nc = c.nc
    es = contextlib.ExitStack()
    sem_names = set()
    for e in Prog.ENGS:
        for item in c.prog.q[e]:
            if item[0] == "w":
                sem_names.add(item[1])
            elif item[2] is not None:
                sem_names.add(item[2])
    sems = {n: es.enter_context(nc.semaphore(n)) for n in sorted(sem_names)}
    block = es.enter_context(nc.Block())

    def emit(eng, ename):
        for item in c.prog.q[ename]:
            if item[0] == "w":
                eng.wait_ge(sems[item[1]], item[2])
            else:
                _, fn, inc, inc_n = item
                ins = fn(eng)
                if inc is not None:
                    ins.then_inc(sems[inc], inc_n)

    @block.sync
    def _(e):
        emit(e, "sync")

    @block.tensor
    def _(e):
        emit(e, "tensor")

    @block.vector
    def _(e):
        emit(e, "vector")

    @block.scalar
    def _(e):
        emit(e, "scalar")

    @block.gpsimd
    def _(e):
        emit(e, "gpsimd")

    es.close()


def build_program():
    c = build_nc()
    emit_program(c)
    replay(c)
    return c


def kernel(**inputs):
    shared = _host_shared(inputs)
    x = np.asarray(inputs["L0_attn_x"], dtype=np.float32)
    in_maps = []
    for b in range(NCORES):
        m = dict(shared)
        m["x_fm"] = np.ascontiguousarray(x[b].T)
        m["x_tm"] = np.ascontiguousarray(x[b])
        in_maps.append(m)
    c = build_program()
    res = run_bass_kernel_spmd(c.nc, in_maps, core_ids=list(range(NCORES)))
    out = np.stack([np.asarray(r["out"]) for r in res.results], axis=0)
    return out.astype(np.float32)
